# version 06
# speedup vs baseline: 1.2902x; 1.0010x over previous
import dataclasses
from contextlib import ExitStack

import numpy as np
import concourse.bass as bass
import concourse.mybir as mybir
from concourse.bass_utils import run_bass_kernel_spmd

F32 = mybir.dt.float32
BF16 = mybir.dt.bfloat16
I32 = mybir.dt.int32
OP = mybir.AluOpType
AF = mybir.ActivationFunctionType
AX = mybir.AxisListType

NCORES = 8
D = 1024
T = 16384
BT = 1024
W = 2 * (BT + 3)
B0OFF = 3
B1OFF = BT + 6
DCH = 8
DS = 16
INNER = 4096
LN_EPS = 1e-5
MU = float(np.sqrt(0.5))
STD = float(np.sqrt(0.25 * np.pi))
ERF_SC = 1.0 / (STD * np.sqrt(2.0))
C1 = ERF_SC / float(T)
NEGRAW = -1.0e9 / C1

BLOCKS_OF = [(c, 15 - c) for c in range(NCORES)]
TOKTILES = [(B0OFF, 0), (B0OFF + 512, 512), (B1OFF, 1024), (B1OFF + 512, 1536)]


def fdims(ap, dims, off=0):
    return dataclasses.replace(
        ap, ap=list(ap.ap)[:1] + [list(d) for d in dims], offset=ap.offset + off)


def adims(ap, dims, off=0):
    return dataclasses.replace(ap, ap=[list(d) for d in dims], offset=ap.offset + off)


class Chan:
    def __init__(self, kb, engname, name, serial, step):
        self.kb, self.engname, self.name = kb, engname, name
        self.serial, self.step = serial, step
        self.n = 0
        self.q = []
        self.order = []
        self.sem = None
        self.fence_seen = 0

    def pos(self):
        return self.n * self.step

    def tag(self):
        return (self, self.pos())

    def do(self, builder, reads=(), writes=(), extra=()):
        kb = self.kb
        waits = {}

        def addw(t):
            if t is None:
                return
            ch, p = t
            if (ch is self and self.serial) or p <= 0:
                return
            cur = waits.get(id(ch))
            if cur is None or cur[1] < p:
                waits[id(ch)] = (ch, p)

        if self.fence_seen < len(kb.fences):
            for t in kb.fences[-1]:
                addw(t)
            self.fence_seen = len(kb.fences)
        for b in reads:
            addw(b.w)
        for b in writes:
            addw(b.w)
            for t in b.readers:
                addw(t)
        for t in extra:
            addw(t)
        if self.serial and self.n > 0:
            waits[id(self)] = (self, self.pos())
        self.n += 1
        mytag = (self, self.pos())
        kb.stamp += 1
        self.q.append((builder, list(waits.values())))
        self.order.append(kb.stamp)
        for b in reads:
            b.readers.append(mytag)
        for b in writes:
            b.w = mytag
            b.readers = []
        return mytag


class Buf:
    def __init__(self, name=""):
        self.name = name
        self.w = None
        self.readers = []


def bufs(pfx, n):
    return [Buf(f"{pfx}{i}") for i in range(n)]


class KB:
    def __init__(self, nc, es):
        self.nc, self.es = nc, es
        self.stamp = 0
        self.fences = []
        self.chans = []
        self.t = self.chan("tensor", "t", False, 1)
        self.v = self.chan("vector", "v", True, 1)
        self.s = self.chan("scalar", "s", True, 1)
        self.g = self.chan("gpsimd", "g", True, 1)
        self.dg = self.chan("gpsimd", "dg", False, 16)
        self._dslots = {}

    def dslot(self, name, eng="sync"):
        if name not in self._dslots:
            self._dslots[name] = self.chan(eng, "d_" + name, False, 16)
        return self._dslots[name]

    def chan(self, engname, name, serial, step):
        c = Chan(self, engname, name, serial, step)
        self.chans.append(c)
        return c

    def fence(self):
        cur = [c.tag() for c in self.chans if c.n > 0]
        nops = []
        for c in (self.t, self.v, self.s, self.g):
            nops.append(c.do(lambda e: e.nop(), extra=cur))
        self.fences.append(nops)

    def sbuf(self, name, shape, dt):
        return self.es.enter_context(self.nc.sbuf_tensor(name, shape, dt))

    def build(self, block):
        for c in self.chans:
            c.sem = self.es.enter_context(self.nc.semaphore(f"sem_{c.name}"))
        by_eng = {}
        for c in self.chans:
            by_eng.setdefault(c.engname, []).append(c)

        def emitter(engname):
            def run(eng):
                items = []
                for c in by_eng[engname]:
                    for i, (b, wts) in enumerate(c.q):
                        items.append((c.order[i], c, b, wts))
                items.sort(key=lambda x: x[0])
                for _, c, bld, wts in items:
                    for (ch, p) in wts[1:]:
                        eng.wait_ge(ch.sem, p)
                    instr = bld(eng)
                    if wts:
                        instr._wait_ge(wts[0][0].sem, wts[0][1])
                    instr.then_inc(c.sem, c.step)
            return run

        for engname in ("sync", "gpsimd", "vector", "scalar", "tensor"):
            if engname in by_eng:
                getattr(block, engname)(emitter(engname))


def host_constants(ln_g, ln_b):
    import ml_dtypes
    BF = ml_dtypes.bfloat16
    cstf = np.zeros((128, 272), dtype=np.float32)
    cstf[:, 0:128] = np.eye(128, dtype=np.float32)
    cstf[:, 128:256] = 1.0
    cstf[:, 256:264] = np.asarray(ln_g, np.float32).reshape(8, 128).T
    cstf[:, 264:272] = np.asarray(ln_b, np.float32).reshape(8, 128).T
    cstb = np.zeros((128, 1216), dtype=np.float32)
    cstb[:, 0:128] = 1.0
    m = np.zeros((128, 256), dtype=np.float32)
    for tau in range(128):
        for s in range(16):
            m[tau, s * 16 + (tau // 8)] = 1.0
    cstb[:, 128:384] = m
    m = np.zeros((128, 64), dtype=np.float32)
    for a in range(8):
        for s in range(16):
            m[a * 16 + s, a * 8:(a + 1) * 8] = 1.0
    cstb[:, 384:448] = m
    m = np.zeros((128, 128), dtype=np.float32)
    for a in range(8):
        for s in range(16):
            m[s, a * 16 + s] = 1.0
    cstb[:, 448:576] = m
    pm = np.zeros((8, 8), dtype=np.float32)
    for t_ in range(8):
        pm[t_, :t_ + 1] = 1.0
    cstb2 = np.zeros((128, 1216), dtype=np.float32)
    cstb2[:, 0:1088] = cstb[:, 0:1088]
    cstb2[:, 576:1088] = np.tile(pm.reshape(-1), 8)[None, :]
    cstb2[:, 1088:1216] = np.eye(128, dtype=np.float32)
    return {"cstf": cstf, "cstb": cstb2.astype(BF)}


def build_nc(upto="full"):
    nc = bass.Bass("TRN2", target_bir_lowering=False, debug=False)
    es = ExitStack()
    k = KB(nc, es)
    L1S = ("l1prep", "dqkv", "datt", "full")
    run_mamba = upto in ("mamba0", "dcs", "dy", "dh", "dsg", "dyc", "dh2") or upto in L1S
    run_post = upto in ("mamba0", "dyc", "dh2") or upto in L1S
    run_ffn0 = upto == "ffn0" or upto in L1S
    run_l1prep = upto in L1S
    run_qkv = upto in ("dqkv", "datt", "full")
    run_att = upto in ("datt", "full")
    run_final = upto == "full"

    def din(name, shape, dt=F32):
        return nc.dram_tensor(name, list(shape), dt, kind="ExternalInput").ap()

    x_in = din("x_sl", (2 * BT + 6, D))
    out = nc.dram_tensor("out", [2 * BT, D], F32, kind="ExternalOutput").ap()
    cstf_in = din("cstf", (128, 272))
    cstb_in = din("cstb", (128, 1216), BF16)
    w_sl = {}
    for L in (0, 1):
        w_sl[(L, "g")] = din(f"L{L}gT_sl", (128, INNER))
        w_sl[(L, "u")] = din(f"L{L}uT_sl", (128, INNER))
        w_sl[(L, "d")] = din(f"L{L}dT_sl", (INNER // 8, D))
    stats_in = nc.dram_tensor("stats_in", [1, 8], F32, kind="Internal")
    stats_out = nc.dram_tensor("stats_out", [1, 8], F32, kind="Internal")
    tg_in, tg_out = {}, {}
    for L in (0, 1):
        for wn in ("g", "u"):
            tg_in[(L, wn)] = nc.dram_tensor(f"tin{L}{wn}", [128, INNER], BF16, kind="Internal")
            tg_out[(L, wn)] = nc.dram_tensor(f"tout{L}{wn}", [D, INNER], BF16, kind="Internal")
        tg_in[(L, "d")] = nc.dram_tensor(f"tin{L}d", [INNER // 8, D], BF16, kind="Internal")
        tg_out[(L, "d")] = nc.dram_tensor(f"tout{L}d", [INNER, D], BF16, kind="Internal")
    xpark = nc.dram_tensor("xpark", [2 * BT, D], F32, kind="Internal")
    xnpark = nc.dram_tensor("xnpark", [DCH * 128, 2048], F32, kind="Internal")
    rpark = nc.dram_tensor("rpark", [DCH * 128, W], BF16, kind="Internal")
    xnnpark = nc.dram_tensor("xnnpark", [DCH * 128, W], BF16, kind="Internal")
    w_in_d = din("W_in", (D, 2 * D))
    dt_w_d = din("dt_w", (D, D))
    bc_w_d = din("BC_w", (D, 32))
    w_out_d = din("W_out", (D, D))
    mvec_d = din("mvec", (128, 64))
    ohot_d = din("ohot", (128, 32))
    sg_in = nc.dram_tensor("sg_in", [34, D], F32, kind="Internal")
    sg_out = nc.dram_tensor("sg_out", [NCORES * 34, D], F32, kind="Internal")
    hpark = nc.dram_tensor("hpark", [2 * DCH * 128, 2048], F32, kind="Internal")
    cspark = nc.dram_tensor("cspark", [2 * DCH * 128, BT], F32, kind="Internal")
    ypark = nc.dram_tensor("ypark", [2 * DCH * 128, BT], F32, kind="Internal")
    szpark = nc.dram_tensor("szpark", [2 * DCH * 128, BT], BF16, kind="Internal")
    cpark = nc.dram_tensor("cpark", [32, BT], BF16, kind="Internal")
    ffnout = nc.dram_tensor("ffnout", [DCH * 128, 2048], F32, kind="Internal")
    ygpark = nc.dram_tensor("ygpark", [2 * DCH * 128, BT], F32, kind="Internal")
    h2park = nc.dram_tensor("h2park", [128, 2064], F32, kind="Internal")
    a8park = nc.dram_tensor("a8park", [128, 1024], F32, kind="Internal")
    htbpark = nc.dram_tensor("htbpark", [128, 16 * 128], F32, kind="Internal")
    cdgpark = nc.dram_tensor("cdgpark", [128, 16 * 64], F32, kind="Internal")
    tm5park = nc.dram_tensor("tm5park", [128, 1024], F32, kind="Internal")
    pspark = nc.dram_tensor("pspark", [128, 1024], F32, kind="Internal")
    qpark = nc.dram_tensor("qpark", [128, 2048], BF16, kind="Internal")
    kin = nc.dram_tensor("kin", [128, 2048], BF16, kind="Internal")
    kgath = nc.dram_tensor("kgath", [NCORES * 128, 2048], BF16, kind="Internal")
    vin = nc.dram_tensor("vin", [2 * BT, D], BF16, kind="Internal")
    vgath = nc.dram_tensor("vgath", [NCORES * 2 * BT, D], BF16, kind="Internal")
    x1park = nc.dram_tensor("x1park", [DCH * 128, 2048], F32, kind="Internal")
    avpark = nc.dram_tensor("avpark", [2 * BT, D], F32, kind="Internal")
    ffn1out = nc.dram_tensor("ffn1out", [DCH * 128, 2048], F32, kind="Internal")
    wqk_in = din("Wqk", (D, 128))
    wv_in = din("Wv", (D, D))
    qkvec_in = din("qkvec", (128, 8))
    bvrow_in = din("bvrow", (1, D))
    relrow_in = din("relrow", (1, 32))
    bmasks_in = din("bmasks", (128, 65 * 128), BF16)
    battb_in = din("batt_base", (128, 1536))
    battf_in = din("batt_far", (128, 1536), BF16)
    tsel_in = din("tsel", (128, 48))
    mixout = nc.dram_tensor("mixout", [DCH * 128, 2048], F32, kind="Internal")

    with es:
        cstf = k.sbuf("cstf_s", [128, 272], F32)
        cstb = k.sbuf("cstb_s", [128, 1216], BF16)
        ident = cstf[:, 0:128]
        ones_f = cstf[:, 128:256]
        gb_t = cstf[:, 256:272]
        ones_b = cstb[:, 0:128]
        bmask = cstb[:, 128:384]
        cmask = cstb[:, 384:448]
        srep = cstb[:, 448:576]
        pmask = cstb[:, 576:1088]
        identb = cstb[:, 1088:1216]
        ldtmp = k.sbuf("ldtmp", [128, 1024], F32)
        rows = k.sbuf("rows", [1, 3 * W + 3 * 512], F32)
        eps_t = k.sbuf("eps_t", [128, 1], F32)
        mw_rows = k.sbuf("mw_rows", [1, 16], F32)
        mw_bc = k.sbuf("mw_bc", [128, 16], F32)
        pb = [es.enter_context(nc.psum_tensor(f"pb{i}", [128, 512], F32)) for i in range(8)]

        b_cst = Buf("cst")
        b_ident = b_ones = b_onesb = b_gbt = b_cst
        b_ldtmp, b_rows, b_eps = Buf("ldtmp"), Buf("rows"), Buf("eps")
        b_mwrows, b_mwbc = Buf("mwr"), Buf("mwb")
        b_pb = bufs("pb", 8)
        RS = [0, W, 2 * W, 3 * W, 3 * W + 512, 3 * W + 1024]

        dcst = k.dslot("cst")
        dcst.do(lambda e: e.dma_start(out=cstf[:, :], in_=cstf_in[:, :]), writes=[b_cst])
        dcst.do(lambda e: e.dma_start(out=cstb[:, :], in_=cstb_in[:, :]), writes=[b_cst])
        k.v.do(lambda e: e.memset(eps_t[:, 0:1], LN_EPS), writes=[b_eps])

        worder = [(0, "g"), (0, "u"), (0, "d"), (1, "g"), (1, "u"), (1, "d")]
        with ExitStack() as scA:
            wq_sb2 = [scA.enter_context(nc.sbuf_tensor(f"wq_sb{i}", [128, INNER], F32))
                      for i in range(2)]
            wq_lt = scA.enter_context(nc.sbuf_tensor("wq_lt", [128, INNER], F32))
            wq_t = scA.enter_context(nc.sbuf_tensor("wq_t", [128, INNER], BF16))
            wq_red = scA.enter_context(nc.sbuf_tensor("wq_red", [128, 8], F32))
            b_wq, b_wqlt, b_wqt, b_wqred = Buf("wq"), Buf("wqlt"), Buf("wqt"), Buf("wqred")

            def wslice_ap(L, wn):
                a = w_sl[(L, wn)]
                if wn == "d":
                    return adims(a, [[D, 128], [128 * D, 4], [1, D]])
                return a[:, :]

            dwq = [k.dslot("wqa"), k.dslot("wqb")]
            b_wq2 = bufs("wq", 2)
            for wi, (L, wn) in enumerate(worder):
                wqs = wq_sb2[wi % 2]
                dwq[wi % 2].do(lambda e, L=L, wn=wn, wqs=wqs: e.dma_start(
                    out=wqs[:, :], in_=wslice_ap(L, wn)),
                    writes=[b_wq2[wi % 2]])
                k.v.do(lambda e, wi=wi, wqs=wqs: e.reduce_sum(
                    wq_red[:, wi:wi + 1], wqs[:, :],
                    axis=AX.X, apply_absolute_value=True),
                    reads=[b_wq2[wi % 2]], writes=[b_wqred])
            for wi in range(6):
                k.t.do(lambda e, wi=wi: e.matmul(pb[6][0:1, wi:wi + 1], ones_f[:, 0:1],
                                                 wq_red[:, wi:wi + 1], start=(wi == 0),
                                                 stop=(wi == 5)),
                       reads=[b_wqred, b_ones], writes=[b_pb[6]])
            k.v.do(lambda e: e.tensor_copy(mw_rows[0:1, 0:6], pb[6][0:1, 0:6]),
                   reads=[b_pb[6]], writes=[b_mwrows])
            k.v.do(lambda e: e.memset(mw_rows[0:1, 6:8], 0.0), writes=[b_mwrows])
            k.dg.do(lambda e: e.dma_start(out=stats_in.ap(), in_=mw_rows[0:1, 0:8]),
                    reads=[b_mwrows])
            b_statso = Buf("statso")
            k.g.do(lambda e: e.collective_compute(
                "AllReduce", OP.add, replica_groups=[list(range(NCORES))],
                ins=[stats_in.ap().opt()], outs=[stats_out.ap().opt()]),
                extra=[k.dg.tag()], writes=[b_statso])
            k.dg.do(lambda e: e.dma_start(out=mw_rows[0:1, 8:16], in_=stats_out.ap()),
                    reads=[b_statso], writes=[b_mwrows])
            k.v.do(lambda e: e.tensor_scalar(mw_rows[0:1, 8:14], mw_rows[0:1, 8:14],
                                             1.0 / (INNER * D), 1e-5, OP.mult, OP.max),
                   reads=[b_mwrows], writes=[b_mwrows])
            k.v.do(lambda e: e.tensor_scalar_mul(mw_rows[0:1, 0:6], mw_rows[0:1, 8:14], 0.5),
                   reads=[b_mwrows], writes=[b_mwrows])
            k.t.do(lambda e: e.matmul(pb[6][0:128, 0:6], ones_f[0:1, 0:128],
                                      mw_rows[0:1, 8:14], start=True, stop=False),
                   reads=[b_mwrows, b_ones], writes=[b_pb[6]])
            k.t.do(lambda e: e.matmul(pb[6][0:128, 6:12], ones_f[0:1, 0:128],
                                      mw_rows[0:1, 0:6], start=False, stop=True),
                   reads=[b_mwrows, b_ones], writes=[b_pb[6]])
            k.v.do(lambda e: e.tensor_copy(mw_bc[:, 0:12], pb[6][0:128, 0:12]),
                   reads=[b_pb[6]], writes=[b_mwbc])
            for wi, (L, wn) in enumerate(worder):
                wqs = wq_sb2[wi % 2]
                dwq[wi % 2].do(lambda e, L=L, wn=wn, wqs=wqs: e.dma_start(
                    out=wqs[:, :], in_=wslice_ap(L, wn)),
                    writes=[b_wq2[wi % 2]])
                k.v.do(lambda e, wi=wi, wqs=wqs: e.tensor_scalar(
                    wq_lt[:, :], wqs[:, :], -1.0, mw_bc[:, 6 + wi:7 + wi],
                    OP.mult, OP.is_gt),
                    reads=[b_wq2[wi % 2], b_mwbc], writes=[b_wqlt])
                k.v.do(lambda e, wi=wi, wqs=wqs: e.scalar_tensor_tensor(
                    wq_t[:, :], wqs[:, :], mw_bc[:, 6 + wi:7 + wi], wq_lt[:, :],
                    OP.is_gt, OP.subtract),
                    reads=[b_wq2[wi % 2], b_mwbc, b_wqlt], writes=[b_wqt])
                if wn == "d":
                    dst = adims(tg_in[(L, wn)].ap(), [[D, 128], [128 * D, 4], [1, D]])
                else:
                    dst = tg_in[(L, wn)].ap()[:, :]
                k.dg.do(lambda e, dst=dst: e.dma_start(out=dst, in_=wq_t[:, :]), reads=[b_wqt])
            b_tg = Buf("tg")
            tgdg = k.dg.tag()
        k.fence()
        for kk in worder:
            k.g.do(lambda e, kk=kk: e.collective_compute(
                "AllGather", OP.bypass, replica_groups=[list(range(NCORES))],
                ins=[tg_in[kk].ap().opt()], outs=[tg_out[kk].ap().opt()]),
                extra=[tgdg], writes=[b_tg])

        def layernorm(src_bufs, src, dst_bufs, dst, scr, b_scr, ncols=W, gaff=None,
                      SM=None, SV=None, ST=None):
            SM = RS[0] if SM is None else SM
            SV = RS[1] if SV is None else SV
            ST = RS[2] if ST is None else ST
            nsl = (ncols + 511) // 512
            for sl in range(nsl):
                c0 = sl * 512
                n = min(512, ncols - c0)
                for ch in range(DCH):
                    k.t.do(lambda e, ch=ch, c0=c0, n=n: e.matmul(
                        pb[6][0:1, 0:n], ones_f[:, 0:1], src[ch][:, c0:c0 + n],
                        start=(ch == 0), stop=(ch == DCH - 1)),
                        reads=[src_bufs[ch], b_ones], writes=[b_pb[6]])
                for ch in range(DCH):
                    k.s.do(lambda e, ch=ch, c0=c0, n=n: e.activation(
                        ldtmp[:, 0:n], src[ch][:, c0:c0 + n], AF.Square),
                        reads=[src_bufs[ch]], writes=[b_ldtmp])
                    k.t.do(lambda e, c0=c0, n=n, ch=ch: e.matmul(
                        pb[7][0:1, 0:n], ones_f[:, 0:1], ldtmp[:, 0:n],
                        start=(ch == 0), stop=(ch == DCH - 1)),
                        reads=[b_ldtmp, b_ones], writes=[b_pb[7]])
                k.v.do(lambda e, c0=c0, n=n: e.tensor_copy(
                    rows[0:1, SM + c0:SM + c0 + n], pb[6][0:1, 0:n]),
                    reads=[b_pb[6]], writes=[b_rows])
                k.v.do(lambda e, c0=c0, n=n: e.tensor_copy(
                    rows[0:1, SV + c0:SV + c0 + n], pb[7][0:1, 0:n]),
                    reads=[b_pb[7]], writes=[b_rows])
            k.v.do(lambda e: e.tensor_scalar_mul(rows[0:1, SM:SM + ncols],
                                                 rows[0:1, SM:SM + ncols], 1.0 / D),
                   reads=[b_rows], writes=[b_rows])
            k.v.do(lambda e: e.tensor_scalar_mul(rows[0:1, SV:SV + ncols],
                                                 rows[0:1, SV:SV + ncols], 1.0 / D),
                   reads=[b_rows], writes=[b_rows])
            k.v.do(lambda e: e.tensor_tensor(rows[0:1, ST:ST + ncols],
                                             rows[0:1, SM:SM + ncols],
                                             rows[0:1, SM:SM + ncols], OP.mult),
                   reads=[b_rows], writes=[b_rows])
            k.v.do(lambda e: e.tensor_tensor(rows[0:1, SV:SV + ncols],
                                             rows[0:1, SV:SV + ncols],
                                             rows[0:1, ST:ST + ncols], OP.subtract),
                   reads=[b_rows], writes=[b_rows])
            k.s.do(lambda e: e.activation(rows[0:1, SV:SV + ncols],
                                          rows[0:1, SV:SV + ncols], AF.Ln,
                                          bias=eps_t[0:1, 0:1], scale=1.0),
                   reads=[b_rows, b_eps], writes=[b_rows])
            k.s.do(lambda e: e.activation(rows[0:1, SV:SV + ncols],
                                          rows[0:1, SV:SV + ncols], AF.Exp,
                                          bias=0.0, scale=-0.5),
                   reads=[b_rows], writes=[b_rows])
            k.v.do(lambda e: e.tensor_tensor(rows[0:1, SM:SM + ncols],
                                             rows[0:1, SM:SM + ncols],
                                             rows[0:1, SV:SV + ncols], OP.mult),
                   reads=[b_rows], writes=[b_rows])
            for sl in range(nsl):
                c0 = sl * 512
                n = min(512, ncols - c0)
                k.t.do(lambda e, c0=c0, n=n: e.matmul(
                    pb[6][0:128, 0:n], ones_f[0:1, 0:128], rows[0:1, SM + c0:SM + c0 + n],
                    start=True, stop=True), reads=[b_rows, b_ones], writes=[b_pb[6]])
                k.v.do(lambda e, n=n: e.tensor_copy(scr[:, 512:512 + n], pb[6][0:128, 0:n]),
                       reads=[b_pb[6]], writes=[b_scr])
                k.t.do(lambda e, c0=c0, n=n: e.matmul(
                    pb[7][0:128, 0:n], ones_f[0:1, 0:128], rows[0:1, SV + c0:SV + c0 + n],
                    start=True, stop=True), reads=[b_rows, b_ones], writes=[b_pb[7]])
                k.v.do(lambda e, n=n: e.tensor_copy(scr[:, 0:n], pb[7][0:128, 0:n]),
                       reads=[b_pb[7]], writes=[b_scr])
                for ch in range(DCH):
                    if gaff is None:
                        k.v.do(lambda e, ch=ch, c0=c0, n=n: e.tensor_tensor(
                            dst[ch][:, c0:c0 + n], src[ch][:, c0:c0 + n], scr[:, 0:n],
                            OP.mult),
                            reads=[src_bufs[ch], b_scr], writes=[dst_bufs[ch]])
                        k.v.do(lambda e, ch=ch, c0=c0, n=n: e.tensor_tensor(
                            dst[ch][:, c0:c0 + n], dst[ch][:, c0:c0 + n],
                            scr[:, 512:512 + n], OP.subtract),
                            reads=[b_scr], writes=[dst_bufs[ch]])
                    else:
                        k.v.do(lambda e, ch=ch, c0=c0, n=n: e.tensor_tensor(
                            scr[:, 1024:1024 + n], src[ch][:, c0:c0 + n], scr[:, 0:n],
                            OP.mult),
                            reads=[src_bufs[ch]], writes=[b_scr])
                        k.v.do(lambda e, ch=ch, c0=c0, n=n: e.tensor_tensor(
                            scr[:, 1024:1024 + n], scr[:, 1024:1024 + n],
                            scr[:, 512:512 + n], OP.subtract),
                            writes=[b_scr])
                        k.v.do(lambda e, ch=ch, c0=c0, n=n: e.tensor_scalar(
                            dst[ch][:, c0:c0 + n], scr[:, 1024:1024 + n],
                            gb_t[:, ch:ch + 1], gb_t[:, 8 + ch:9 + ch], OP.mult, OP.add),
                            reads=[b_scr, b_gbt], writes=[dst_bufs[ch]])

        def act_quant(src_bufs, src, dst_bufs, dst, sslot, islot, sA, b_sA, sF, b_sF,
                      sI, b_sI, ncols=W):
            nsl0 = (ncols + 511) // 512
            for sl in range(nsl0):
                c0 = sl * 512
                n = min(512, ncols - c0)
                for ch in range(DCH):
                    if ch == 0:
                        k.s.do(lambda e, c0=c0, n=n: e.activation(
                            sA[:, c0:c0 + n], src[0][:, c0:c0 + n], AF.Abs),
                            reads=[src_bufs[0]], writes=[b_sA])
                    else:
                        k.s.do(lambda e, ch=ch, c0=c0, n=n: e.activation(
                            sF[:, 0:n], src[ch][:, c0:c0 + n], AF.Abs),
                            reads=[src_bufs[ch]], writes=[b_sF])
                        k.v.do(lambda e, c0=c0, n=n: e.tensor_tensor(
                            sA[:, c0:c0 + n], sA[:, c0:c0 + n], sF[:, 0:n], OP.max),
                            reads=[b_sF], writes=[b_sA])
            ntile = (ncols + 127) // 128
            for ti in range(ntile):
                c0 = ti * 128
                n = min(128, ncols - c0)
                k.t.do(lambda e, c0=c0, n=n: e.transpose(
                    pb[7][0:n, 0:128], sA[:, c0:c0 + n], ident[:, :]),
                    reads=[b_sA, b_ident], writes=[b_pb[7]])
                k.v.do(lambda e, n=n: e.reduce_max(sF[0:n, 0:1], pb[7][0:n, 0:128], axis=AX.X),
                       reads=[b_pb[7]], writes=[b_sF])
                k.t.do(lambda e, n=n: e.transpose(
                    pb[7][0:1, 0:n], sF[0:n, 0:1], ident[0:n, 0:n]),
                    reads=[b_sF, b_ident], writes=[b_pb[7]])
                k.v.do(lambda e, c0=c0, n=n: e.tensor_copy(
                    rows[0:1, sslot + c0:sslot + c0 + n], pb[7][0:1, 0:n]),
                    reads=[b_pb[7]], writes=[b_rows])
            k.v.do(lambda e: e.tensor_scalar_max(rows[0:1, sslot:sslot + ncols],
                                                 rows[0:1, sslot:sslot + ncols], 1e-5),
                   reads=[b_rows], writes=[b_rows])
            k.v.do(lambda e: e.tensor_scalar_mul(rows[0:1, islot:islot + ncols],
                                                 rows[0:1, sslot:sslot + ncols], 1.0 / 127.0),
                   reads=[b_rows], writes=[b_rows])
            k.v.do(lambda e: e.reciprocal(rows[0:1, sslot:sslot + ncols],
                                          rows[0:1, sslot:sslot + ncols]),
                   reads=[b_rows], writes=[b_rows])
            k.v.do(lambda e: e.tensor_scalar_mul(rows[0:1, sslot:sslot + ncols],
                                                 rows[0:1, sslot:sslot + ncols], 127.0),
                   reads=[b_rows], writes=[b_rows])
            nsl = (ncols + 511) // 512
            for sl in range(nsl):
                c0 = sl * 512
                n = min(512, ncols - c0)
                k.t.do(lambda e, c0=c0, n=n: e.matmul(
                    pb[7][0:128, 0:n], ones_f[0:1, 0:128],
                    rows[0:1, sslot + c0:sslot + c0 + n], start=True, stop=True),
                    reads=[b_rows, b_ones], writes=[b_pb[7]])
                k.v.do(lambda e, c0=c0, n=n: e.tensor_copy(sA[:, c0:c0 + n], pb[7][0:128, 0:n]),
                       reads=[b_pb[7]], writes=[b_sA])
            for ch in range(DCH):
                for sl in range(nsl):
                    c0 = sl * 512
                    n = min(512, ncols - c0)
                    k.v.do(lambda e, ch=ch, c0=c0, n=n: e.tensor_tensor(
                        sF[:, 0:n], src[ch][:, c0:c0 + n], sA[:, c0:c0 + n], OP.mult),
                        reads=[src_bufs[ch], b_sA], writes=[b_sF])
                    k.v.do(lambda e, n=n: e.tensor_copy(sI[:, 0:n], sF[:, 0:n]),
                           reads=[b_sF], writes=[b_sI])
                    k.v.do(lambda e, ch=ch, c0=c0, n=n: e.tensor_copy(
                        dst[ch][:, c0:c0 + n], sI[:, 0:n]),
                        reads=[b_sI], writes=[dst_bufs[ch]])

        b_park = Buf("xpark")
        b_xnnp = Buf("xnnpark")
        b_xnp = Buf("xnp")
        b_rp = Buf("rpark")
        with ExitStack() as sc1:
            xnf = [sc1.enter_context(nc.sbuf_tensor(f"xnf{i}", [128, W], F32))
                   for i in range(DCH)]
            xnn_bf = [sc1.enter_context(nc.sbuf_tensor(f"xnn{i}", [128, W], BF16))
                      for i in range(DCH)]
            r_bf = [sc1.enter_context(nc.sbuf_tensor(f"rbf{i}", [128, W], BF16))
                    for i in range(DCH)]
            b_xnn = bufs("xnn", DCH)
            b_rbf = bufs("rbf", DCH)
            ld_a = sc1.enter_context(nc.sbuf_tensor("ld_a", [128, 1024], F32))
            ld_b = sc1.enter_context(nc.sbuf_tensor("ld_b", [128, 1024], F32))
            sqt = sc1.enter_context(nc.sbuf_tensor("sqt", [128, 1024], F32))
            stt = sc1.enter_context(nc.sbuf_tensor("stt", [128, 8], F32))
            rmini = sc1.enter_context(nc.sbuf_tensor("rmini", [1, 256], F32))
            scrB = sc1.enter_context(nc.sbuf_tensor("scrB", [128, W], F32))
            scrF = sc1.enter_context(nc.sbuf_tensor("scrF", [128, 512], F32))
            scrI = sc1.enter_context(nc.sbuf_tensor("scrI", [128, 512], I32))
            b_xnf = bufs("xnf", DCH)
            b_lda, b_ldb = Buf("lda"), Buf("ldb")
            b_sqt, b_stt, b_rmini = Buf("sqt"), Buf("stt"), Buf("rmini")
            b_scrB, b_scrF, b_scrI = Buf("sB"), Buf("sF"), Buf("sI")

            tiles = []
            for blk in range(2):
                ro = blk * (BT + 3)
                co = blk * (BT + 3)
                tiles.append((ro, 3, co, None))
                for ti in range(8):
                    tiles.append((ro + 3 + ti * 128, 128, co + 3 + ti * 128,
                                  blk * 1024 + ti * 128))
            dlds = [k.dslot("lda"), k.dslot("ldb")]
            dpks = [k.dslot("pka"), k.dslot("pkb")]
            for idx, (ro, n, co, prow) in enumerate(tiles):
                ld, bld = (ld_a, b_lda) if idx % 2 == 0 else (ld_b, b_ldb)
                dlds[idx % 2].do(lambda e, ro=ro, n=n, ld=ld: e.dma_start(
                    out=ld[0:n, :], in_=x_in[ro:ro + n, :]), writes=[bld])
                if prow is not None:
                    dpks[idx % 2].do(lambda e, prow=prow, n=n, ld=ld: e.dma_start(
                        out=xpark.ap()[prow:prow + n, :], in_=ld[0:n, :]),
                        reads=[bld], writes=[b_park])
                k.s.do(lambda e, n=n, ld=ld: e.activation(sqt[0:n, :], ld[0:n, :], AF.Square),
                       reads=[bld], writes=[b_sqt])
                k.v.do(lambda e, n=n, ld=ld: e.reduce_sum(stt[0:n, 0:1], ld[0:n, :], axis=AX.X),
                       reads=[bld], writes=[b_stt])
                k.v.do(lambda e, n=n: e.reduce_sum(stt[0:n, 1:2], sqt[0:n, :], axis=AX.X),
                       reads=[b_sqt], writes=[b_stt])
                k.v.do(lambda e, n=n: e.tensor_scalar_mul(stt[0:n, 2:3], stt[0:n, 0:1],
                                                          1.0 / D),
                       reads=[b_stt], writes=[b_stt])
                k.v.do(lambda e, n=n: e.tensor_scalar_mul(stt[0:n, 3:4], stt[0:n, 1:2],
                                                          1.0 / D),
                       reads=[b_stt], writes=[b_stt])
                k.v.do(lambda e, n=n: e.tensor_tensor(stt[0:n, 4:5], stt[0:n, 2:3],
                                                      stt[0:n, 2:3], OP.mult),
                       reads=[b_stt], writes=[b_stt])
                k.v.do(lambda e, n=n: e.tensor_tensor(stt[0:n, 3:4], stt[0:n, 3:4],
                                                      stt[0:n, 4:5], OP.subtract),
                       reads=[b_stt], writes=[b_stt])
                k.s.do(lambda e, n=n: e.activation(stt[0:n, 3:4], stt[0:n, 3:4], AF.Ln,
                                                   bias=eps_t[0:n, 0:1], scale=1.0),
                       reads=[b_stt, b_eps], writes=[b_stt])
                k.s.do(lambda e, n=n: e.activation(stt[0:n, 3:4], stt[0:n, 3:4], AF.Exp,
                                                   bias=0.0, scale=-0.5),
                       reads=[b_stt], writes=[b_stt])
                k.v.do(lambda e, n=n: e.tensor_tensor(stt[0:n, 2:3], stt[0:n, 2:3],
                                                      stt[0:n, 3:4], OP.mult),
                       reads=[b_stt], writes=[b_stt])
                k.t.do(lambda e, n=n: e.transpose(pb[6][0:1, 0:n], stt[0:n, 3:4],
                                                  ident[0:n, 0:n]),
                       reads=[b_stt, b_ident], writes=[b_pb[6]])
                k.t.do(lambda e, n=n: e.transpose(pb[6][0:1, 128:128 + n], stt[0:n, 2:3],
                                                  ident[0:n, 0:n]),
                       reads=[b_stt, b_ident], writes=[b_pb[6]])
                k.v.do(lambda e, n=n: e.tensor_copy(rmini[0:1, 0:n], pb[6][0:1, 0:n]),
                       reads=[b_pb[6]], writes=[b_rmini])
                k.v.do(lambda e, n=n: e.tensor_copy(rmini[0:1, 128:128 + n],
                                                    pb[6][0:1, 128:128 + n]),
                       reads=[b_pb[6]], writes=[b_rmini])
                k.t.do(lambda e, n=n: e.matmul(pb[6][0:128, 0:n], ones_f[0:1, 0:128],
                                               rmini[0:1, 0:n], start=True, stop=True),
                       reads=[b_rmini, b_ones], writes=[b_pb[6]])
                k.t.do(lambda e, n=n: e.matmul(pb[6][0:128, 128:128 + n], ones_f[0:1, 0:128],
                                               rmini[0:1, 128:128 + n], start=True, stop=True),
                       reads=[b_rmini, b_ones], writes=[b_pb[6]])
                k.v.do(lambda e, n=n: e.tensor_copy(scrF[:, 0:n], pb[6][0:128, 0:n]),
                       reads=[b_pb[6]], writes=[b_scrF])
                k.v.do(lambda e, n=n: e.tensor_copy(scrF[:, 128:128 + n],
                                                    pb[6][0:128, 128:128 + n]),
                       reads=[b_pb[6]], writes=[b_scrF])
                for ch in range(DCH):
                    pi = ch % 2
                    k.t.do(lambda e, ch=ch, pi=pi, n=n, ld=ld: e.transpose(
                        pb[pi][0:128, 0:n], ld[0:n, ch * 128:(ch + 1) * 128],
                        ident[0:n, 0:n]),
                        reads=[bld, b_ident], writes=[b_pb[pi]])
                    k.v.do(lambda e, ch=ch, pi=pi, n=n, co=co: e.tensor_tensor(
                        xnf[ch][:, co:co + n], pb[pi][0:128, 0:n], scrF[:, 0:n], OP.mult),
                        reads=[b_pb[pi], b_scrF], writes=[b_xnf[ch]])
                    k.v.do(lambda e, ch=ch, n=n, co=co: e.tensor_tensor(
                        xnf[ch][:, co:co + n], xnf[ch][:, co:co + n],
                        scrF[:, 128:128 + n], OP.subtract),
                        reads=[b_scrF], writes=[b_xnf[ch]])
            layernorm(b_xnf, xnf, b_xnn, xnn_bf, scrB, b_scrB, gaff=True)
            act_quant(b_xnf, xnf, b_rbf, r_bf, RS[0], RS[1], scrB, b_scrB, scrF, b_scrF,
                      scrI, b_scrI)
            dxnp = k.dslot("xnp")
            drp = k.dslot("rp")
            dxnnp = k.dslot("xnnp")
            for ch in range(DCH):
                for blk in range(2):
                    co = B0OFF if blk == 0 else B1OFF
                    dxnp.do(lambda e, ch=ch, co=co, blk=blk: e.dma_start(
                        out=xnpark.ap()[ch * 128:(ch + 1) * 128, blk * 1024:(blk + 1) * 1024],
                        in_=xnf[ch][:, co:co + 1024]),
                        reads=[b_xnf[ch]], writes=[b_xnp])
                drp.do(lambda e, ch=ch: e.dma_start(
                    out=rpark.ap()[ch * 128:(ch + 1) * 128, :], in_=r_bf[ch][:, :]),
                    reads=[b_rbf[ch]], writes=[b_rp])
                dxnnp.do(lambda e, ch=ch: e.dma_start(
                    out=xnnpark.ap()[ch * 128:(ch + 1) * 128, :], in_=xnn_bf[ch][:, :]),
                    reads=[b_xnn[ch]], writes=[b_xnnp])
            if upto == "ln1":
                dso = k.dslot("so")
                for blk in range(2):
                    co = blk * (BT + 3) + 3
                    for ti in range(8):
                        for ch in range(DCH):
                            pi = 4 + ch % 2
                            k.t.do(lambda e, ch=ch, pi=pi, c=co + ti * 128: e.transpose(
                                pb[pi][0:128, 0:128], xnf[ch][:, c:c + 128], ident[:, :]),
                                reads=[b_xnf[ch], b_ident], writes=[b_pb[pi]])
                            k.v.do(lambda e, ch=ch, pi=pi: e.tensor_copy(
                                ldtmp[:, ch * 128:(ch + 1) * 128], pb[pi][0:128, 0:128]),
                                reads=[b_pb[pi]], writes=[b_ldtmp])
                        dso.do(lambda e, ro=blk * BT + ti * 128: e.dma_start(
                            out=out[ro:ro + 128, :], in_=ldtmp[:, :]), reads=[b_ldtmp])
            k.fence()


        def mamba():
            b_bc = Buf("bcsb")
            b_cpk = Buf("cpk")
            b_mvec = Buf("mvec")
            b_sgin = Buf("sgin")
            b_hp, b_csp, b_yp, b_szp = Buf("hp"), Buf("csp"), Buf("yp"), Buf("szp")
            with ExitStack() as sm0:
                bc_sb = sm0.enter_context(nc.sbuf_tensor("bc_sb", [128, 256], BF16))
                mvec = sm0.enter_context(nc.sbuf_tensor("mvec_s", [128, 64], F32))
                a_t = sm0.enter_context(nc.sbuf_tensor("a_t", [128, 8], F32))
                b_at = Buf("a_t")
                dbc = k.dslot("bc", "gpsimd")
                dbc.do(lambda e: e.dma_start(out=bc_sb[:, :], in_=adims(
                    bc_w_d, [[32, 128], [128 * 32, 8], [1, 32]])), writes=[b_bc])
                dmv = k.dslot("cst")
                dmv.do(lambda e: e.dma_start(out=mvec[:, :], in_=mvec_d[:, :]),
                       writes=[b_mvec])
                k.s.do(lambda e: e.activation(a_t[:, :], mvec[:, 24:32], AF.Exp),
                       reads=[b_mvec], writes=[b_at])
                k.v.do(lambda e: e.tensor_scalar_mul(a_t[:, :], a_t[:, :], -1.0),
                       reads=[b_at], writes=[b_at])

                Bfm = sm0.enter_context(nc.sbuf_tensor("Bfm", [16, 2048], F32))
                Cfm = sm0.enter_context(nc.sbuf_tensor("Cfm", [16, 2048], F32))
                Cbf = sm0.enter_context(nc.sbuf_tensor("Cbf", [16, 2048], BF16))
                b_Bfm, b_Cfm, b_Cbf = Buf("Bfm"), Buf("Cfm"), Buf("Cbf")

                rf = sm0.enter_context(nc.sbuf_tensor("rf_s", [128, BT], F32))
                rfb = sm0.enter_context(nc.sbuf_tensor("rfb_s", [128, BT], F32))
                r8tm = sm0.enter_context(nc.sbuf_tensor("r8tm_s", [128, 1024], BF16))
                btm = sm0.enter_context(nc.sbuf_tensor("btm_s", [128, 128], BF16))
                bdg = sm0.enter_context(nc.sbuf_tensor("bdg_s", [128, 256], BF16))
                usc = sm0.enter_context(nc.sbuf_tensor("usc_s", [128, 2048], F32))
                psc = sm0.enter_context(nc.sbuf_tensor("psc_s", [128, 2048], F32))
                hsc = sm0.enter_context(nc.sbuf_tensor("hsc_s", [128, 2048], F32))
                pr8 = sm0.enter_context(nc.sbuf_tensor("pr8_s", [128, 128], F32))
                onesw = sm0.enter_context(nc.sbuf_tensor("onesw_s", [128, BT], F32))
                cbt = sm0.enter_context(nc.sbuf_tensor("cbt_s", [16, 512], BF16))
                pwt2 = [sm0.enter_context(nc.sbuf_tensor(f"pwt_s{i}", [128, 512], F32))
                        for i in range(2)]
                pwb2 = [sm0.enter_context(nc.sbuf_tensor(f"pwb_s{i}", [128, 512], BF16))
                        for i in range(2)]
                pwr2 = [sm0.enter_context(nc.sbuf_tensor(f"pwr_s{i}", [128, 64], F32))
                        for i in range(2)]
                sg16 = sm0.enter_context(nc.sbuf_tensor("sg16_s", [16, 128], F32))
                sg1 = sm0.enter_context(nc.sbuf_tensor("sg1_s", [128, 1], F32))
                sg1r = sm0.enter_context(nc.sbuf_tensor("sg1r_s", [1, 128], F32))
                b_rf, b_rfb, b_r8tm = Buf("rf"), Buf("rfb"), Buf("r8tm")
                b_btm, b_bdg = Buf("btm"), Buf("bdg")
                b_usc, b_psc, b_hsc, b_pr8 = Buf("usc"), Buf("psc"), Buf("hsc"), Buf("pr8")
                b_onesw = Buf("onesw")
                b_cbt = Buf("cbt")
                b_pwt2 = bufs("pwt", 2)
                b_pwb2 = bufs("pwb", 2)
                b_pwr2 = bufs("pwr", 2)
                b_sg16, b_sg1, b_sg1r = Buf("sg16"), Buf("sg1"), Buf("sg1r")
                k.v.do(lambda e: e.memset(onesw[:, :], 1.0), writes=[b_onesw])

                def build_btm(blk):
                    for g in range(8):
                        k.t.do(lambda e, g=g, blk=blk: e.transpose(
                            pb[6][0:128, 0:16],
                            Bfm[0:16, blk * 1024 + g * 128:blk * 1024 + (g + 1) * 128],
                            ident[0:16, 0:16]),
                            reads=[b_Bfm, b_cst], writes=[b_pb[6]])
                        k.v.do(lambda e, g=g: e.tensor_copy(
                            btm[:, g * 16:(g + 1) * 16], pb[6][0:128, 0:16]),
                            reads=[b_pb[6]], writes=[b_btm])

                def mamba_block(blk):
                    boff = B0OFF if blk == 0 else B1OFF
                    with ExitStack() as smb:
                        xssm = [smb.enter_context(nc.sbuf_tensor(
                            f"xs{blk}_{i}", [128, BT], BF16)) for i in range(DCH)]
                        b_xssm = bufs("xssm", DCH)
                        szr = [smb.enter_context(nc.sbuf_tensor(
                            f"sz{blk}_{i}", [128, BT], BF16)) for i in range(2)]
                        b_szr = bufs("szr", 2)
                        dszp = [k.dslot("szpa"), k.dslot("szpb")]

                        with ExitStack() as sm1:
                            xnn_l = [sm1.enter_context(nc.sbuf_tensor(
                                f"xnl{blk}_{i}", [128, 1027], BF16)) for i in range(DCH)]
                            b_xnnl = bufs("xnnl", DCH)
                            win_a = sm1.enter_context(nc.sbuf_tensor(f"win{blk}a", [128, 4096], BF16))
                            win_b = sm1.enter_context(nc.sbuf_tensor(f"win{blk}b", [128, 4096], BF16))
                            b_wina, b_winb = Buf("wina"), Buf("winb")
                            xz0_a = sm1.enter_context(nc.sbuf_tensor(f"xz0{blk}a", [128, 1027], F32))
                            xz0_b = sm1.enter_context(nc.sbuf_tensor(f"xz0{blk}b", [128, 1027], F32))
                            b_xz0a, b_xz0b = Buf("xz0a"), Buf("xz0b")
                            cvt = sm1.enter_context(nc.sbuf_tensor(f"cvt{blk}", [128, 1024], F32))
                            b_cvt = Buf("cvt")
                            sfm = sm1.enter_context(nc.sbuf_tensor(f"sfm{blk}", [128, 512], F32))
                            b_sfm = Buf("sfm")

                            dxnn = k.dslot("xnn")
                            for ch in range(DCH):
                                dxnn.do(lambda e, ch=ch: e.dma_start(
                                    out=xnn_l[ch][:, 0:1027],
                                    in_=xnnpark.ap()[ch * 128:(ch + 1) * 128, boff - 3:boff + 1024]),
                                    extra=[k.dslot("xnnp").tag()], writes=[b_xnnl[ch]])
                            xt_all = dxnn.tag()
                            for ch in range(DCH):
                                b_xnnl[ch].w = xt_all
                            dwin = [k.dslot("wina", "gpsimd"), k.dslot("winb", "gpsimd")]

                            for osl in range(4):
                                wt, bwt = (win_a, b_wina) if osl % 2 == 0 else (win_b, b_winb)
                                srcap = adims(w_in_d, [[2 * D, 128], [128 * 2 * D, 8], [1, 512]],
                                              off=osl * 512)
                                dwin[osl % 2].do(lambda e, srcap=srcap, wt=wt: e.dma_start(
                                    out=wt[:, :], in_=srcap), writes=[bwt])
                                for oc in range(4):
                                    o0 = osl * 512 + oc * 128
                                    is_x = o0 < 1024
                                    nsls = ((0, 512), (512, 512), (1024, 3)) if is_x else \
                                           ((3, 512), (515, 512))
                                    for (nc0, nn) in nsls:
                                        bank = b_pb[oc % 4]
                                        psv = pb[oc % 4]
                                        for h in range(DCH):
                                            st, sp = (h == 0), (h == DCH - 1)
                                            wr = [bank] if (st or sp) else []
                                            k.t.do(lambda e, wt=wt, oc=oc, h=h, psv=psv, nc0=nc0,
                                                   nn=nn, st=st, sp=sp: e.matmul(
                                                       psv[0:128, 0:nn],
                                                       wt[:, h * 512 + oc * 128:h * 512 + oc * 128 + 128],
                                                       xnn_l[h][:, nc0:nc0 + nn], start=st, stop=sp),
                                                   reads=[bwt, b_xnnl[h]], writes=wr)
                                        if is_x:
                                            ch = o0 // 128
                                            xz, bxz = (xz0_a, b_xz0a) if ch % 2 == 0 else (xz0_b, b_xz0b)
                                            k.v.do(lambda e, xz=xz, nc0=nc0, nn=nn, psv=psv:
                                                   e.tensor_copy(xz[:, nc0:nc0 + nn],
                                                                 psv[0:128, 0:nn]),
                                                   reads=[bank], writes=[bxz])
                                        else:
                                            zch = (o0 - 1024) // 128
                                            k.s.do(lambda e, zch=zch, nc0=nc0, nn=nn, psv=psv:
                                                   e.activation(szr[zch % 2][:, nc0 - 3:nc0 - 3 + nn],
                                                                psv[0:128, 0:nn], AF.Silu),
                                                   reads=[bank], writes=[b_szr[zch % 2]])
                                    if is_x:
                                        ch = o0 // 128
                                        xz, bxz = (xz0_a, b_xz0a) if ch % 2 == 0 else (xz0_b, b_xz0b)
                                        k.v.do(lambda e, xz=xz, ch=ch: e.tensor_scalar(
                                            cvt[:, :], xz[:, 0:1024], mvec[:, 32 + ch * 4:33 + ch * 4],
                                            None, OP.mult),
                                            reads=[bxz, b_mvec], writes=[b_cvt])
                                        for tap in range(1, 4):
                                            k.v.do(lambda e, xz=xz, ch=ch, tap=tap:
                                                   e.scalar_tensor_tensor(
                                                       cvt[:, :], xz[:, tap:tap + 1024],
                                                       mvec[:, 32 + ch * 4 + tap:33 + ch * 4 + tap],
                                                       cvt[:, :], OP.mult, OP.add),
                                                   reads=[bxz, b_mvec], writes=[b_cvt])
                                        for half in range(2):
                                            h0 = half * 512
                                            k.s.do(lambda e, ch=ch, h0=h0: e.activation(
                                                sfm[:, :], cvt[:, h0:h0 + 512], AF.Silu,
                                                bias=mvec[:, ch:ch + 1], scale=1.0),
                                                reads=[b_cvt, b_mvec], writes=[b_sfm])
                                            k.v.do(lambda e, ch=ch, h0=h0: e.tensor_copy(
                                                xssm[ch][:, h0:h0 + 512], sfm[:, :]),
                                                reads=[b_sfm], writes=[b_xssm[ch]])
                                    else:
                                        zch = (o0 - 1024) // 128
                                        dszp[zch % 2].do(lambda e, blk=blk, zch=zch: e.dma_start(
                                            out=szpark.ap()[(blk * DCH + zch) * 128:
                                                            (blk * DCH + zch + 1) * 128, :],
                                            in_=szr[zch % 2][:, :]),
                                            reads=[b_szr[zch % 2]], writes=[b_szp])

                        k.fence()
                        for which in range(2):
                            dst = Bfm if which == 0 else Cfm
                            bdst = b_Bfm if which == 0 else b_Cfm
                            for (nc0, nn) in ((0, 512), (512, 512)):
                                for h in range(DCH):
                                    st, sp = (h == 0), (h == DCH - 1)
                                    wr = [b_pb[5]] if (st or sp) else []
                                    k.t.do(lambda e, which=which, h=h, nc0=nc0, nn=nn,
                                           st=st, sp=sp: e.matmul(
                                               pb[5][0:16, 0:nn],
                                               bc_sb[:, h * 32 + which * 16:h * 32 + which * 16 + 16],
                                               xssm[h][:, nc0:nc0 + nn], start=st, stop=sp),
                                           reads=[b_bc, b_xssm[h]], writes=wr)
                                k.v.do(lambda e, dst=dst, blk=blk, nc0=nc0, nn=nn: e.tensor_copy(
                                    dst[0:16, blk * 1024 + nc0:blk * 1024 + nc0 + nn],
                                    pb[5][0:16, 0:nn]),
                                    reads=[b_pb[5]], writes=[bdst])
                        k.v.do(lambda e, blk=blk: e.tensor_copy(
                            Cbf[0:16, blk * 1024:(blk + 1) * 1024],
                            Cfm[0:16, blk * 1024:(blk + 1) * 1024]),
                            reads=[b_Cfm], writes=[b_Cbf])
                        build_btm(blk)
                        k.dslot("cpk").do(lambda e, blk=blk: e.dma_start(
                            out=cpark.ap()[blk * 16:(blk + 1) * 16, :],
                            in_=Cbf[0:16, blk * 1024:(blk + 1) * 1024]),
                            reads=[b_Cbf], writes=[b_cpk])

                        with ExitStack() as sm3:
                            wdt_a = sm3.enter_context(nc.sbuf_tensor(f"wdt{blk}a", [128, 4096], BF16))
                            wdt_b = sm3.enter_context(nc.sbuf_tensor(f"wdt{blk}b", [128, 4096], BF16))
                            b_wdta, b_wdtb = Buf("wdta"), Buf("wdtb")
                            sp1 = sm3.enter_context(nc.sbuf_tensor(f"sp1{blk}", [128, 512], F32))
                            sp2 = sm3.enter_context(nc.sbuf_tensor(f"sp2{blk}", [128, 512], F32))
                            b_sp1, b_sp2 = Buf("sp1"), Buf("sp2")
                            dtt = sm3.enter_context(nc.sbuf_tensor(f"dtt{blk}", [128, 1024], F32))
                            b_dtt = Buf("dtt")
                            csr = [sm3.enter_context(nc.sbuf_tensor(
                                f"cs{blk}_{i}", [128, BT], F32)) for i in range(2)]
                            b_csr = bufs("csr", 2)
                            ur = [sm3.enter_context(nc.sbuf_tensor(
                                f"u{blk}_{i}", [128, BT], BF16)) for i in range(2)]
                            b_ur = bufs("ur", 2)
                            yr = [sm3.enter_context(nc.sbuf_tensor(
                                f"y{blk}_{i}", [128, BT], F32)) for i in range(2)]
                            b_yr = bufs("yr", 2)
                            gmt_all = sm3.enter_context(nc.sbuf_tensor(
                                f"gmt{blk}", [128, 8192], BF16))
                            b_gmt = Buf("gmt")
                            for g in range(16):
                                t0 = g * 64
                                k.v.do(lambda e, t0=t0, blk=blk: e.tensor_tensor(
                                    cbt[0:16, :],
                                    fdims(Cfm[0:16, :], [[8, 8], [1, 8], [0, 8]],
                                          off=blk * 1024 + t0),
                                    fdims(Bfm[0:16, :], [[8, 8], [0, 8], [1, 8]],
                                          off=blk * 1024 + t0), OP.mult),
                                    reads=[b_Cfm, b_Bfm], writes=[b_cbt])
                                k.t.do(lambda e: e.matmul(pb[7][0:128, 0:512], ones_b[0:16, :],
                                                          cbt[0:16, :], start=True, stop=True),
                                       reads=[b_cbt, b_cst], writes=[b_pb[7]])
                                k.v.do(lambda e, g=g: e.tensor_tensor(
                                    gmt_all[:, g * 512:(g + 1) * 512], pb[7][0:128, 0:512],
                                    pmask, OP.mult),
                                    reads=[b_pb[7], b_cst], writes=[b_gmt])

                            ddtw = [k.dslot("dtwa", "gpsimd"), k.dslot("dtwb", "gpsimd")]
                            dhp = [k.dslot("hpa"), k.dslot("hpb")]
                            dcsp = [k.dslot("cspa"), k.dslot("cspb")]
                            dyp = [k.dslot("ypa"), k.dslot("ypb")]
                            dsgu = [k.dslot("sgua"), k.dslot("sgub")]
                            dsgp = [k.dslot("sgpa"), k.dslot("sgpb")]
                            for osl in range(2):
                                wt, bwt = (wdt_a, b_wdta) if osl % 2 == 0 else (wdt_b, b_wdtb)
                                srcap = adims(dt_w_d, [[D, 128], [128 * D, 8], [1, 512]],
                                              off=osl * 512)
                                ddtw[osl % 2].do(lambda e, srcap=srcap, wt=wt: e.dma_start(
                                    out=wt[:, :], in_=srcap), writes=[bwt])
                                for oc in range(4):
                                    ch = osl * 4 + oc
                                    par = ch % 2
                                    cs, b_cs = csr[par], b_csr[par]
                                    u, b_u = ur[par], b_ur[par]
                                    yb, b_y = yr[par], b_yr[par]
                                    for (nc0, nn) in ((0, 512), (512, 512)):
                                        bank = b_pb[4]
                                        for h in range(DCH):
                                            st, sp = (h == 0), (h == DCH - 1)
                                            wr = [bank] if (st or sp) else []
                                            k.t.do(lambda e, wt=wt, oc=oc, h=h, nc0=nc0, nn=nn,
                                                   st=st, sp=sp: e.matmul(
                                                       pb[4][0:128, 0:nn],
                                                       wt[:, h * 512 + oc * 128:h * 512 + oc * 128 + 128],
                                                       xssm[h][:, nc0:nc0 + nn], start=st, stop=sp),
                                                   reads=[bwt, b_xssm[h]], writes=wr)
                                        k.s.do(lambda e, ch=ch, nc0=nc0, nn=nn: e.activation(
                                            sp1[:, 0:nn], pb[4][0:128, 0:nn], AF.Abs,
                                            bias=mvec[:, 8 + ch:9 + ch], scale=1.0),
                                            reads=[bank, b_mvec], writes=[b_sp1])
                                        k.s.do(lambda e, nn=nn: e.activation(
                                            sp1[:, 0:nn], sp1[:, 0:nn], AF.Exp, bias=0.0,
                                            scale=-1.0),
                                            reads=[b_sp1], writes=[b_sp1])
                                        k.s.do(lambda e, nn=nn: e.activation(
                                            sp1[:, 0:nn], sp1[:, 0:nn], AF.Ln,
                                            bias=cstf[:, 128:129], scale=1.0),
                                            reads=[b_sp1, b_cst], writes=[b_sp1])
                                        k.s.do(lambda e, ch=ch, nn=nn: e.activation(
                                            sp2[:, 0:nn], pb[4][0:128, 0:nn], AF.Relu,
                                            bias=mvec[:, 8 + ch:9 + ch], scale=1.0),
                                            reads=[bank, b_mvec], writes=[b_sp2])
                                        k.v.do(lambda e, nc0=nc0, nn=nn: e.tensor_tensor(
                                            dtt[:, nc0:nc0 + nn], sp1[:, 0:nn], sp2[:, 0:nn],
                                            OP.add),
                                            reads=[b_sp1, b_sp2], writes=[b_dtt])
                                    k.v.do(lambda e, cs=cs, ch=ch: e.tensor_scalar(
                                        cs[:, :], dtt[:, :], a_t[:, ch:ch + 1], None, OP.mult),
                                        reads=[b_dtt, b_at], writes=[b_cs])
                                    k.v.do(lambda e, u=u, ch=ch: e.tensor_tensor(
                                        u[:, :], dtt[:, :], xssm[ch][:, :], OP.mult),
                                        reads=[b_dtt, b_xssm[ch]], writes=[b_u])
                                    k.v.do(lambda e, yb=yb, ch=ch: e.tensor_scalar(
                                        yb[:, :], xssm[ch][:, :], mvec[:, 16 + ch:17 + ch],
                                        None, OP.mult),
                                        reads=[b_xssm[ch], b_mvec], writes=[b_y])
                                    k.v.do(lambda e, cs=cs: e.tensor_tensor_scan(
                                        cs[:, :], onesw[:, :], cs[:, :], 0.0,
                                        OP.mult, OP.add),
                                        reads=[b_onesw], writes=[b_cs])

                                    k.v.do(lambda e, cs=cs: e.tensor_tensor(
                                        rf[:, :], fdims(cs[:, :], [[8, 128], [0, 8]], off=7),
                                        cs[:, :], OP.subtract),
                                        reads=[b_cs], writes=[b_rf])
                                    k.s.do(lambda e: e.activation(rf[:, :], rf[:, :], AF.Exp),
                                           reads=[b_rf], writes=[b_rf])
                                    k.v.do(lambda e, u=u: e.tensor_tensor(
                                        rfb[:, :], rf[:, :], u[:, :], OP.mult),
                                        reads=[b_rf, b_u], writes=[b_rfb])
                                    for g in range(8):
                                        k.t.do(lambda e, g=g: e.transpose(
                                            pb[6][0:128, 0:128], rfb[:, g * 128:(g + 1) * 128], ident),
                                            reads=[b_rfb, b_cst], writes=[b_pb[6]])
                                        k.v.do(lambda e, g=g: e.tensor_copy(
                                            r8tm[:, g * 128:(g + 1) * 128], pb[6][0:128, 0:128]),
                                            reads=[b_pb[6]], writes=[b_r8tm])
                                    for g in range(8):
                                        k.v.do(lambda e, g=g: e.tensor_tensor(
                                            bdg[:, :], fdims(btm[:, g * 16:(g + 1) * 16],
                                                             [[1, 16], [0, 16]]),
                                            bmask, OP.mult),
                                            reads=[b_btm, b_cst], writes=[b_bdg])
                                        k.t.do(lambda e, g=g: e.matmul(
                                            pb[7][0:128, 0:256], r8tm[:, g * 128:(g + 1) * 128],
                                            bdg[:, :], start=True, stop=True),
                                            reads=[b_r8tm, b_bdg], writes=[b_pb[7]])
                                        k.v.do(lambda e, g=g: e.tensor_copy(
                                            fdims(usc[:, :], [[128, 16], [1, 16]], off=g * 16),
                                            pb[7][0:128, 0:256]),
                                            reads=[b_pb[7]], writes=[b_usc])
                                    k.v.do(lambda e, cs=cs: e.tensor_tensor(
                                        pr8[:, 1:128], fdims(cs[:, :], [[8, 127], [0, 1]], off=15),
                                        fdims(cs[:, :], [[8, 127], [0, 1]], off=7), OP.subtract),
                                        reads=[b_cs], writes=[b_pr8])
                                    k.s.do(lambda e: e.activation(pr8[:, 1:128], pr8[:, 1:128], AF.Exp),
                                           reads=[b_pr8], writes=[b_pr8])
                                    k.v.do(lambda e: e.memset(pr8[:, 0:1], 0.0), writes=[b_pr8])
                                    k.v.do(lambda e: e.tensor_copy(
                                        psc[:, :], fdims(pr8[:, 0:128], [[0, 16], [1, 128]])),
                                        reads=[b_pr8], writes=[b_psc])
                                    k.v.do(lambda e: e.tensor_tensor_scan(
                                        hsc[:, :], psc[:, :], usc[:, :], 0.0, OP.mult, OP.add),
                                        reads=[b_psc, b_usc], writes=[b_hsc])
                                    dhp[par].do(lambda e, blk=blk, ch=ch: e.dma_start(
                                        out=hpark.ap()[(blk * DCH + ch) * 128:(blk * DCH + ch + 1) * 128, :],
                                        in_=hsc[:, :]), reads=[b_hsc], writes=[b_hp])
                                    k.t.do(lambda e: e.transpose(
                                        pb[6][0:16, 0:128],
                                        fdims(hsc[:, :], [[128, 16]], off=127), ident[:, :]),
                                        reads=[b_hsc, b_cst], writes=[b_pb[6]])
                                    k.v.do(lambda e: e.tensor_copy(sg16[0:16, 0:128], pb[6][0:16, 0:128]),
                                           reads=[b_pb[6]], writes=[b_sg16])
                                    dsgu[par].do(lambda e, blk=blk, ch=ch: e.dma_start(
                                        out=sg_in.ap()[blk * 17:blk * 17 + 16, ch * 128:(ch + 1) * 128],
                                        in_=sg16[0:16, 0:128]), reads=[b_sg16], writes=[b_sgin])
                                    k.s.do(lambda e, cs=cs: e.activation(
                                        sg1[:, 0:1], cs[:, 1023:1024], AF.Exp),
                                        reads=[b_cs], writes=[b_sg1])
                                    k.t.do(lambda e: e.transpose(pb[6][0:1, 0:128], sg1[:, 0:1],
                                                                 ident),
                                           reads=[b_sg1, b_cst], writes=[b_pb[6]])
                                    k.v.do(lambda e: e.tensor_copy(sg1r[0:1, 0:128], pb[6][0:1, 0:128]),
                                           reads=[b_pb[6]], writes=[b_sg1r])
                                    dsgp[par].do(lambda e, blk=blk, ch=ch: e.dma_start(
                                        out=sg_in.ap()[blk * 17 + 16:blk * 17 + 17,
                                                       ch * 128:(ch + 1) * 128],
                                        in_=sg1r[0:1, 0:128]), reads=[b_sg1r], writes=[b_sgin])

                                    for g in range(16):
                                        t0 = g * 64
                                        gp = g % 2
                                        pwtg, b_pwtg = pwt2[gp], b_pwt2[gp]
                                        pwbg, b_pwbg = pwb2[gp], b_pwb2[gp]
                                        pwrg, b_pwrg = pwr2[gp], b_pwr2[gp]
                                        k.v.do(lambda e, cs=cs, t0=t0, pwtg=pwtg:
                                               e.tensor_tensor(
                                                   pwtg[:, :],
                                                   fdims(cs[:, :], [[8, 8], [1, 8], [0, 8]],
                                                         off=t0),
                                                   fdims(cs[:, :], [[8, 8], [0, 8], [1, 8]],
                                                         off=t0),
                                                   OP.subtract),
                                               reads=[b_cs], writes=[b_pwtg])
                                        k.v.do(lambda e, pwtg=pwtg: e.tensor_scalar_min(
                                            pwtg[:, :], pwtg[:, :], 0.0),
                                            reads=[b_pwtg], writes=[b_pwtg])
                                        k.s.do(lambda e, pwtg=pwtg, pwbg=pwbg: e.activation(
                                            pwbg[:, :], pwtg[:, :], AF.Exp),
                                            reads=[b_pwtg], writes=[b_pwbg])
                                        k.v.do(lambda e, g=g, pwbg=pwbg: e.tensor_tensor(
                                            pwbg[:, :], pwbg[:, :],
                                            gmt_all[:, g * 512:(g + 1) * 512], OP.mult),
                                            reads=[b_gmt, b_pwbg], writes=[b_pwbg])
                                        k.v.do(lambda e, u=u, t0=t0, pwbg=pwbg: e.tensor_tensor(
                                            pwbg[:, :], pwbg[:, :],
                                            fdims(u[:, :], [[8, 8], [0, 8], [1, 8]], off=t0),
                                            OP.mult),
                                            reads=[b_u, b_pwbg], writes=[b_pwbg])
                                        k.v.do(lambda e, pwbg=pwbg, pwrg=pwrg: e.tensor_reduce(
                                            pwrg[:, 0:64], fdims(pwbg[:, :], [[8, 64], [1, 8]]),
                                            AX.X, OP.add),
                                            reads=[b_pwbg], writes=[b_pwrg])
                                        k.v.do(lambda e, yb=yb, t0=t0, pwrg=pwrg:
                                               e.tensor_tensor(
                                                   yb[:, t0:t0 + 64], yb[:, t0:t0 + 64],
                                                   pwrg[:, 0:64], OP.add),
                                               reads=[b_pwrg], writes=[b_y])
                                    dcsp[par].do(lambda e, blk=blk, ch=ch, cs=cs: e.dma_start(
                                        out=cspark.ap()[(blk * DCH + ch) * 128:(blk * DCH + ch + 1) * 128, :],
                                        in_=cs[:, :]), reads=[b_cs], writes=[b_csp])
                                    dyp[par].do(lambda e, blk=blk, ch=ch, yb=yb: e.dma_start(
                                        out=ypark.ap()[(blk * DCH + ch) * 128:(blk * DCH + ch + 1) * 128, :],
                                        in_=yb[:, :]), reads=[b_y], writes=[b_yp])

                for blk in range(2):
                    mamba_block(blk)
                    k.fence()
                b_sgo = Buf("sgo")
                k.g.do(lambda e: e.collective_compute(
                    "AllGather", OP.bypass, replica_groups=[list(range(NCORES))],
                    ins=[sg_in.ap().opt()], outs=[sg_out.ap().opt()]),
                    extra=[k.dslot("sgua").tag(), k.dslot("sgub").tag(),
                           k.dslot("sgpa").tag(), k.dslot("sgpb").tag()], writes=[b_sgo])
                k.fence()
                return b_sgo

        if run_mamba:
            b_sgo = mamba()

        def mamba_post(b_sgo):
            b_mix = Buf("mix")
            with ExitStack() as sp0:
                ohs = sp0.enter_context(nc.sbuf_tensor("ohs", [128, 32], F32))
                ub16 = sp0.enter_context(nc.sbuf_tensor("ub16", [16, 1024], F32))
                pstage = sp0.enter_context(nc.sbuf_tensor("pstage", [1, 1024], F32))
                pbc = sp0.enter_context(nc.sbuf_tensor("pbc", [16, 1024], F32))
                hg = sp0.enter_context(nc.sbuf_tensor("hg", [16, 1024], F32))
                hin = [sp0.enter_context(nc.sbuf_tensor(f"hin{i}", [16, 1024], F32))
                       for i in range(2)]
                hint = [sp0.enter_context(nc.sbuf_tensor(f"hint{i}", [128, 128], F32))
                        for i in range(2)]
                crl = sp0.enter_context(nc.sbuf_tensor("crl", [16, 2048], BF16))
                hrl_a = sp0.enter_context(nc.sbuf_tensor("hrl_a", [128, 2048], F32))
                hrl_b = sp0.enter_context(nc.sbuf_tensor("hrl_b", [128, 2048], F32))
                csl_a = sp0.enter_context(nc.sbuf_tensor("csl_a", [128, 1024], F32))
                csl_b = sp0.enter_context(nc.sbuf_tensor("csl_b", [128, 1024], F32))
                yl_a = sp0.enter_context(nc.sbuf_tensor("yl_a", [128, 1024], F32))
                yl_b = sp0.enter_context(nc.sbuf_tensor("yl_b", [128, 1024], F32))
                szl_a = sp0.enter_context(nc.sbuf_tensor("szl_a", [128, 1024], BF16))
                szl_b = sp0.enter_context(nc.sbuf_tensor("szl_b", [128, 1024], BF16))
                h2 = sp0.enter_context(nc.sbuf_tensor("h2", [128, 2064], F32))
                corr = sp0.enter_context(nc.sbuf_tensor("corr", [128, 2048], F32))
                e8 = sp0.enter_context(nc.sbuf_tensor("e8", [128, 128], F32))
                a8 = sp0.enter_context(nc.sbuf_tensor("a8", [128, 1024], F32))
                htb = sp0.enter_context(nc.sbuf_tensor("htb", [128, 128], BF16))
                cdg = sp0.enter_context(nc.sbuf_tensor("cdg", [128, 64], BF16))
                tmp5 = sp0.enter_context(nc.sbuf_tensor("tmp5", [128, 512], F32))
                ygb = [sp0.enter_context(nc.sbuf_tensor(f"ygb{i}", [128, 2048], BF16))
                       for i in range(DCH)]
                wo_a = sp0.enter_context(nc.sbuf_tensor("wo_a", [128, 4096], BF16))
                wo_b = sp0.enter_context(nc.sbuf_tensor("wo_b", [128, 4096], BF16))
                xnl = sp0.enter_context(nc.sbuf_tensor("xnl", [128, 512], F32))
                mixs = sp0.enter_context(nc.sbuf_tensor("mixs", [128, 512], F32))
                b_ohs, b_ub, b_pst, b_pbc, b_hg = (Buf("ohs"), Buf("ub"), Buf("pst"),
                                                   Buf("pbc"), Buf("hg"))
                b_hin = bufs("hin", 2)
                b_hint = bufs("hint", 2)
                b_crl, b_hrla, b_hrlb = Buf("crl"), Buf("hrla"), Buf("hrlb")
                b_csla, b_cslb, b_yla, b_ylb = Buf("csla"), Buf("cslb"), Buf("yla"), Buf("ylb")
                b_szla, b_szlb = Buf("szla"), Buf("szlb")
                b_h2, b_corr, b_e8, b_a8 = Buf("h2"), Buf("corr"), Buf("e8"), Buf("a8")
                b_htb, b_cdg, b_tmp5 = Buf("htb"), Buf("cdg"), Buf("tmp5")
                b_ygb = bufs("ygb", DCH)
                b_woa, b_wob = Buf("woa"), Buf("wob")
                b_xnl, b_mixs = Buf("xnl"), Buf("mixs")

                doh = k.dslot("cst")
                doh.do(lambda e: e.dma_start(out=ohs[:, :], in_=ohot_d[:, :]), writes=[b_ohs])
                dsgu = k.dslot("sgu")
                dsgp = k.dslot("sgp")
                k.v.do(lambda e: e.memset(hg[:, :], 0.0), writes=[b_hg])
                k.v.do(lambda e: e.memset(hin[0][:, :], 0.0), writes=[b_hin[0]])
                k.v.do(lambda e: e.memset(hin[1][:, :], 0.0), writes=[b_hin[1]])
                for b in range(16):
                    c_b = min(b, 15 - b)
                    slot = 0 if b == c_b else 1
                    r0 = c_b * 34 + slot * 17
                    for i in range(2):
                        k.v.do(lambda e, b=b, i=i: e.scalar_tensor_tensor(
                            hin[i][:, :], hg[:, :], ohs[0:16, i * 16 + b:i * 16 + b + 1],
                            hin[i][:, :], OP.mult, OP.add),
                            reads=[b_hg, b_ohs], writes=[b_hin[i]])
                    dsgu.do(lambda e, r0=r0: e.dma_start(
                        out=ub16[:, :], in_=sg_out.ap()[r0:r0 + 16, :]),
                        extra=[b_sgo.w] if b_sgo.w else [], writes=[b_ub])
                    dsgp.do(lambda e, r0=r0: e.dma_start(
                        out=pstage[:, :], in_=sg_out.ap()[r0 + 16:r0 + 17, :]),
                        extra=[b_sgo.w] if b_sgo.w else [], writes=[b_pst])
                    for half in range(2):
                        k.t.do(lambda e, half=half: e.matmul(
                            pb[6][0:16, 0:512],
                            ones_f[0:1, 0:16], pstage[0:1, half * 512:(half + 1) * 512],
                            start=True, stop=True),
                            reads=[b_pst, b_cst], writes=[b_pb[6]])
                        k.v.do(lambda e, half=half: e.tensor_copy(
                            pbc[0:16, half * 512:(half + 1) * 512], pb[6][0:16, 0:512]),
                            reads=[b_pb[6]], writes=[b_pbc])
                    k.v.do(lambda e: e.tensor_tensor(hg[:, :], hg[:, :], pbc[:, :], OP.mult),
                           reads=[b_pbc], writes=[b_hg])
                    k.v.do(lambda e: e.tensor_tensor(hg[:, :], hg[:, :], ub16[:, :], OP.add),
                           reads=[b_ub], writes=[b_hg])
                for i in range(2):
                    for dg in range(DCH):
                        k.t.do(lambda e, i=i, dg=dg: e.transpose(
                            pb[6][0:128, 0:16], hin[i][0:16, dg * 128:(dg + 1) * 128],
                            ident[0:16, 0:16]),
                            reads=[b_hin[i], b_cst], writes=[b_pb[6]])
                        k.v.do(lambda e, i=i, dg=dg: e.tensor_copy(
                            hint[i][:, dg * 16:(dg + 1) * 16], pb[6][0:128, 0:16]),
                            reads=[b_pb[6]], writes=[b_hint[i]])
                dcr = k.dslot("crl")
                dcr.do(lambda e: e.dma_start(out=crl[0:16, 0:1024], in_=cpark.ap()[0:16, :]),
                       extra=[k.dslot("cpk").tag()], writes=[b_crl])
                dcr.do(lambda e: e.dma_start(out=crl[0:16, 1024:2048], in_=cpark.ap()[16:32, :]),
                       extra=[k.dslot("cpk").tag()], writes=[b_crl])

                dh = [k.dslot("hrla"), k.dslot("hrlb")]
                dc = [k.dslot("csla"), k.dslot("cslb")]
                dy = [k.dslot("yla"), k.dslot("ylb")]
                dz = [k.dslot("szla"), k.dslot("szlb")]
                idx = 0
                for blk in range(2):
                    for dg in range(DCH):
                        par = idx % 2
                        hrl, b_hrl = (hrl_a, b_hrla) if par == 0 else (hrl_b, b_hrlb)
                        csl, b_csl = (csl_a, b_csla) if par == 0 else (csl_b, b_cslb)
                        yl, b_yl = (yl_a, b_yla) if par == 0 else (yl_b, b_ylb)
                        szl, b_szl = (szl_a, b_szla) if par == 0 else (szl_b, b_szlb)
                        row = (blk * DCH + dg) * 128
                        dh[par].do(lambda e, row=row, hrl=hrl: e.dma_start(
                            out=hrl[:, :], in_=hpark.ap()[row:row + 128, :]),
                            extra=[k.dslot("hpa").tag(), k.dslot("hpb").tag()],
                            writes=[b_hrl])
                        dc[par].do(lambda e, row=row, csl=csl: e.dma_start(
                            out=csl[:, :], in_=cspark.ap()[row:row + 128, :]),
                            extra=[k.dslot("cspa").tag(), k.dslot("cspb").tag()],
                            writes=[b_csl])
                        dy[par].do(lambda e, row=row, yl=yl: e.dma_start(
                            out=yl[:, :], in_=ypark.ap()[row:row + 128, :]),
                            extra=[k.dslot("ypa").tag(), k.dslot("ypb").tag()],
                            writes=[b_yl])
                        dz[par].do(lambda e, row=row, szl=szl: e.dma_start(
                            out=szl[:, :], in_=szpark.ap()[row:row + 128, :]),
                            extra=[k.dslot("szpa").tag(), k.dslot("szpb").tag()],
                            writes=[b_szl])
                        idx += 1
                        k.s.do(lambda e, csl=csl: e.activation(
                            e8[:, :], fdims(csl[:, :], [[8, 128]], off=7), AF.Exp),
                            reads=[b_csl], writes=[b_e8])
                        k.v.do(lambda e, hrl=hrl: e.tensor_copy(
                            h2[:, 16:2064], fdims(hrl[:, :], [[1, 128], [128, 16]])),
                            reads=[b_hrl], writes=[b_h2])
                        k.v.do(lambda e, blk=blk, dg=dg: e.tensor_copy(
                            h2[:, 0:16], hint[blk][:, dg * 16:(dg + 1) * 16]),
                            reads=[b_hint[blk]], writes=[b_h2])
                        k.v.do(lambda e, blk=blk, dg=dg: e.tensor_tensor(
                            corr[:, :], fdims(hint[blk][:, :], [[0, 128], [1, 16]],
                                              off=dg * 16),
                            fdims(e8[:, :], [[1, 128], [0, 16]]), OP.mult),
                            reads=[b_hint[blk], b_e8], writes=[b_corr])
                        k.v.do(lambda e: e.tensor_tensor(
                            h2[:, 16:2064], h2[:, 16:2064], corr[:, :], OP.add),
                            reads=[b_corr], writes=[b_h2])
                        if upto == "dh2" and blk == 0 and dg == 0:
                            k.dslot("dbh2").do(lambda e: e.dma_start(
                                out=h2park.ap()[:, :], in_=h2[:, :]), reads=[b_h2])
                        k.s.do(lambda e, csl=csl: e.activation(
                            a8[:, 0:8], csl[:, 0:8], AF.Exp),
                            reads=[b_csl], writes=[b_a8])
                        k.v.do(lambda e, csl=csl: e.tensor_tensor(
                            a8[:, 8:1024], csl[:, 8:1024],
                            fdims(csl[:, :], [[8, 127], [0, 8]], off=7), OP.subtract),
                            reads=[b_csl], writes=[b_a8])
                        k.s.do(lambda e: e.activation(a8[:, 8:1024], a8[:, 8:1024], AF.Exp),
                               reads=[b_a8], writes=[b_a8])
                        for half in range(2):
                            bankh = b_pb[2 + half]
                            for g8 in range(8):
                                g = half * 8 + g8
                                k.t.do(lambda e, g=g: e.transpose(
                                    pb[1][0:128, 0:128], h2[:, g * 128:(g + 1) * 128],
                                    ident),
                                    reads=[b_h2, b_cst], writes=[b_pb[1]])
                                k.v.do(lambda e: e.tensor_copy(htb[:, :], pb[1][0:128, 0:128]),
                                       reads=[b_pb[1]], writes=[b_htb])
                                k.t.do(lambda e, blk=blk, g=g: e.matmul(
                                    pb[0][0:128, 0:64], srep[0:16, :],
                                    crl[0:16, blk * 1024 + g * 64:blk * 1024 + (g + 1) * 64],
                                    start=True, stop=True),
                                    reads=[b_crl, b_cst], writes=[b_pb[0]])
                                k.v.do(lambda e: e.tensor_tensor(
                                    cdg[:, :], pb[0][0:128, 0:64], cmask, OP.mult),
                                    reads=[b_pb[0], b_cst], writes=[b_cdg])
                                if upto == "dh2" and blk == 0 and dg == 0:
                                    k.dslot(f"dbhtb{g % 2}", "gpsimd").do(lambda e, g=g: e.dma_start(
                                        out=htbpark.ap()[:, g * 128:(g + 1) * 128],
                                        in_=htb[:, :]), reads=[b_htb])
                                    k.dslot(f"dbcdg{g % 2}", "gpsimd").do(lambda e, g=g: e.dma_start(
                                        out=cdgpark.ap()[:, g * 64:(g + 1) * 64],
                                        in_=cdg[:, :]), reads=[b_cdg])
                                k.t.do(lambda e, g8=g8, half=half: e.matmul(
                                    pb[2 + half][0:128, g8 * 64:(g8 + 1) * 64],
                                    htb[:, :], cdg[:, :],
                                    start=True, stop=True),
                                    reads=[b_htb, b_cdg], writes=[bankh])
                            if upto == "dh2" and blk == 0 and dg == 0:
                                k.v.do(lambda e, half=half: e.tensor_copy(
                                    corr[:, 0:512], pb[2 + half][0:128, 0:512]),
                                    reads=[bankh], writes=[b_corr])
                                k.dslot(f"dbps{half}").do(lambda e, half=half: e.dma_start(
                                    out=pspark.ap()[:, half * 512:(half + 1) * 512],
                                    in_=corr[:, 0:512]), reads=[b_corr])
                            k.v.do(lambda e, half=half: e.tensor_tensor(
                                tmp5[:, :], pb[2 + half][0:128, 0:512],
                                a8[:, half * 512:(half + 1) * 512], OP.mult),
                                reads=[bankh, b_a8], writes=[b_tmp5])
                            if upto == "dh2" and blk == 0 and dg == 0:
                                k.dslot(f"dbt5{half}").do(lambda e, half=half: e.dma_start(
                                    out=tm5park.ap()[:, half * 512:(half + 1) * 512],
                                    in_=tmp5[:, :]), reads=[b_tmp5])
                            k.v.do(lambda e, yl=yl, half=half: e.tensor_tensor(
                                yl[:, half * 512:(half + 1) * 512],
                                yl[:, half * 512:(half + 1) * 512], tmp5[:, :], OP.add),
                                reads=[b_tmp5], writes=[b_yl])
                        if upto == "dyc":
                            k.dslot("ygpa" if par == 0 else "ygpb").do(
                                lambda e, yl=yl, blk=blk, dg=dg: e.dma_start(
                                out=ygpark.ap()[(blk * DCH + dg) * 128:
                                                (blk * DCH + dg + 1) * 128, :],
                                in_=yl[:, :]), reads=[b_yl])
                        if upto == "dh2" and blk == 0 and dg == 0:
                            k.dslot("dba8").do(lambda e: e.dma_start(
                                out=a8park.ap()[:, :], in_=a8[:, :]), reads=[b_a8])
                        k.v.do(lambda e, yl=yl, szl=szl, dg=dg, blk=blk: e.tensor_tensor(
                            ygb[dg][:, blk * 1024:(blk + 1) * 1024], yl[:, :], szl[:, :],
                            OP.mult),
                            reads=[b_yl, b_szl], writes=[b_ygb[dg]])
                dwo = [k.dslot("woa", "gpsimd"), k.dslot("wob", "gpsimd")]
                dxnl = k.dslot("xnl")
                dmix = k.dslot("mix")
                for osl in range(2):
                    wt, bwt = (wo_a, b_woa) if osl % 2 == 0 else (wo_b, b_wob)
                    srcap = adims(w_out_d, [[D, 128], [128 * D, 8], [1, 512]], off=osl * 512)
                    dwo[osl % 2].do(lambda e, srcap=srcap, wt=wt: e.dma_start(
                        out=wt[:, :], in_=srcap), writes=[bwt])
                    for oc in range(4):
                        ch = osl * 4 + oc
                        for nsl in range(4):
                            nc0 = nsl * 512
                            bank = b_pb[nsl % 2]
                            for h in range(DCH):
                                st, sp = (h == 0), (h == DCH - 1)
                                wr = [bank] if (st or sp) else []
                                k.t.do(lambda e, wt=wt, oc=oc, h=h, nc0=nc0, nsl=nsl,
                                       st=st, sp=sp: e.matmul(
                                           pb[nsl % 2][0:128, 0:512],
                                           wt[:, h * 512 + oc * 128:h * 512 + oc * 128 + 128],
                                           ygb[h][:, nc0:nc0 + 512], start=st, stop=sp),
                                       reads=[bwt, b_ygb[h]], writes=wr)
                            dxnl.do(lambda e, ch=ch, nc0=nc0: e.dma_start(
                                out=xnl[:, :],
                                in_=xnpark.ap()[ch * 128:(ch + 1) * 128, nc0:nc0 + 512]),
                                extra=[k.dslot("xnp").tag()], writes=[b_xnl])
                            k.v.do(lambda e, nsl=nsl: e.tensor_tensor(
                                mixs[:, :], pb[nsl % 2][0:128, 0:512], xnl[:, :], OP.add),
                                reads=[bank, b_xnl], writes=[b_mixs])
                            dmix.do(lambda e, ch=ch, nc0=nc0: e.dma_start(
                                out=mixout.ap()[ch * 128:(ch + 1) * 128, nc0:nc0 + 512],
                                in_=mixs[:, :]), reads=[b_mixs], writes=[b_mix])
            k.fence()
            return b_mix

        if run_post:
            b_mix = mamba_post(b_sgo)

        DUMPS = {"dcs": cspark, "dy": ypark, "dh": hpark, "dsg": sg_out,
                 "dyc": ygpark}
        if upto in DUMPS:
            srcT = DUMPS[upto]
            dd = k.dslot("so")
            ext = [b_sgo.w]
            for n in ("cspa", "cspb", "ypa", "ypb", "hpa", "hpb",
                      "ygpa", "ygpb"):
                if n in k._dslots and k._dslots[n].n > 0:
                    ext.append(k._dslots[n].tag())
            if upto == "dsg":
                dd.do(lambda e: e.dma_start(out=out[0:272, :], in_=srcT.ap()[:, :]),
                      extra=ext)
            else:
                for r in range(16):
                    dd.do(lambda e, r=r: e.dma_start(
                        out=out[r * 128:(r + 1) * 128, :],
                        in_=srcT.ap()[r * 128:(r + 1) * 128, 0:1024]), extra=ext)


        if upto == "mamba0":
            with ExitStack() as scd:
                dbg = [scd.enter_context(nc.sbuf_tensor(f"dbgm{i}", [128, 2048], F32))
                       for i in range(DCH)]
                b_dbg = bufs("dbgm", DCH)
                ldt2 = scd.enter_context(nc.sbuf_tensor("ldt2", [128, 1024], F32))
                b_ldt2 = Buf("ldt2")
                ddbg = k.dslot("dbg")
                for ch in range(DCH):
                    ddbg.do(lambda e, ch=ch: e.dma_start(
                        out=dbg[ch][:, :], in_=mixout.ap()[ch * 128:(ch + 1) * 128, :]),
                        extra=[k.dslot("mix").tag()], writes=[b_dbg[ch]])
                dbg_all = ddbg.tag()
                dso = k.dslot("so")
                for blk in range(2):
                    for ti in range(8):
                        rr = blk * 1024 + ti * 128
                        for ch in range(DCH):
                            pi = 4 + ch % 2
                            k.t.do(lambda e, ch=ch, pi=pi, rr=rr: e.transpose(
                                pb[pi][0:128, 0:128], dbg[ch][:, rr:rr + 128], ident),
                                reads=[b_dbg[ch], b_cst], writes=[b_pb[pi]],
                                extra=[dbg_all])
                            k.v.do(lambda e, ch=ch, pi=pi: e.tensor_copy(
                                ldt2[:, ch * 128:(ch + 1) * 128], pb[pi][0:128, 0:128]),
                                reads=[b_pb[pi]], writes=[b_ldt2])
                        dso.do(lambda e, ro=blk * BT + ti * 128: e.dma_start(
                            out=out[ro:ro + 128, :], in_=ldt2[:, :]), reads=[b_ldt2])

        def ffn_pass(L, sslot, islot, outdram, b_outdram):
            with ExitStack() as sc3:
                rb = [sc3.enter_context(nc.sbuf_tensor(f"rl{L}_{i}", [128, W], BF16))
                      for i in range(DCH)]
                rb_bufs = bufs(f"rl{L}", DCH)
                drl = k.dslot(f"rld{L}")
                for ch in range(DCH):
                    drl.do(lambda e, ch=ch: e.dma_start(
                        out=rb[ch][:, :], in_=rpark.ap()[ch * 128:(ch + 1) * 128, :]),
                        extra=[k.dslot("rp").tag()] + ([k.dslot("r1p").tag()] if L == 1
                                                       else []),
                        reads=[b_rp], writes=[rb_bufs[ch]])
                rl_all = drl.tag()
                for ch in range(DCH):
                    rb_bufs[ch].w = rl_all
                gubuf = sc3.enter_context(nc.sbuf_tensor(f"gubuf{L}", [128, 32 * 512], F32))
                wt_a = sc3.enter_context(nc.sbuf_tensor(f"wt{L}a", [128, 4096], BF16))
                wt_b = sc3.enter_context(nc.sbuf_tensor(f"wt{L}b", [128, 4096], BF16))
                wd_a = sc3.enter_context(nc.sbuf_tensor(f"wd{L}a", [128, 1024], BF16))
                wd_b = sc3.enter_context(nc.sbuf_tensor(f"wd{L}b", [128, 1024], BF16))
                guq_a = sc3.enter_context(nc.sbuf_tensor(f"guq{L}a", [128, 512], BF16))
                guq_b = sc3.enter_context(nc.sbuf_tensor(f"guq{L}b", [128, 512], BF16))
                qg_b = sc3.enter_context(nc.sbuf_tensor(f"qg{L}", [128, W], F32))
                sig_b = sc3.enter_context(nc.sbuf_tensor(f"sig{L}", [128, 512], F32))
                osc_b = sc3.enter_context(nc.sbuf_tensor(f"osc{L}", [128, 512], F32))
                sF3 = sc3.enter_context(nc.sbuf_tensor(f"sF3{L}", [128, 512], F32))
                sI3 = sc3.enter_context(nc.sbuf_tensor(f"sI3{L}", [128, 512], I32))
                sB3 = sc3.enter_context(nc.sbuf_tensor(f"sB3{L}", [128, 512], F32))
                b_gub = Buf("gub")
                b_wta, b_wtb, b_wda, b_wdb = Buf("wta"), Buf("wtb"), Buf("wda"), Buf("wdb")
                b_guqa, b_guqb = Buf("guqa"), Buf("guqb")
                b_qgb, b_sigb, b_oscb = Buf("qgb"), Buf("sigb"), Buf("oscb")
                dtmp, b_dtmp = gubuf, b_gub
                b_sF3, b_sI3, b_sB3 = Buf("sF3"), Buf("sI3"), Buf("sB3")
                gi, ui, di_ = 3 * L, 3 * L + 1, 3 * L + 2
                dws = [k.dslot(f"wt{L}a"), k.dslot(f"wt{L}b")]
                dwds = [k.dslot(f"wd{L}a"), k.dslot(f"wd{L}b")]
                dfo = k.dslot(f"fo{L}")

                k.v.do(lambda e: e.tensor_scalar(
                    rows[0:1, RS[2]:RS[2] + W], rows[0:1, islot:islot + W],
                    mw_rows[0:1, 8 + gi:9 + gi], None, OP.mult),
                    reads=[b_rows, b_mwrows], writes=[b_rows])
                for sl in range((W + 511) // 512):
                    c0 = sl * 512
                    n = min(512, W - c0)
                    k.t.do(lambda e, c0=c0, n=n: e.matmul(
                        pb[7][0:128, 0:n], ones_f[0:1, 0:128],
                        rows[0:1, RS[2] + c0:RS[2] + c0 + n], start=True, stop=True),
                        reads=[b_rows, b_ones], writes=[b_pb[7]])
                    k.v.do(lambda e, c0=c0, n=n: e.tensor_copy(
                        qg_b[:, c0:c0 + n], pb[7][0:128, 0:n]),
                        reads=[b_pb[7]], writes=[b_qgb])

                for (tc0, rt0) in TOKTILES:
                    for phase in ("g", "u"):
                        tgd = tg_out[(L, phase)]
                        for IS in range(8):
                            wt, bwt = (wt_a, b_wta) if IS % 2 == 0 else (wt_b, b_wtb)
                            srcap = adims(tgd.ap(),
                                          [[INNER, 128], [128 * INNER, 8], [1, 512]],
                                          off=IS * 512)
                            dws[0 if IS % 2 == 0 else 1].do(
                                lambda e, srcap=srcap, wt=wt: e.dma_start(
                                    out=wt[:, :], in_=srcap), writes=[bwt])
                            for j in range(4):
                                bank, psv = b_pb[j], pb[j]
                                for h in range(DCH):
                                    st, sp = (h == 0), (h == DCH - 1)
                                    wr = [bank] if (st or sp) else []
                                    k.t.do(lambda e, wt=wt, j=j, h=h, psv=psv, tc0=tc0,
                                           st=st, sp=sp:
                                           e.matmul(psv[0:128, 0:512],
                                                    wt[:, h * 512 + j * 128:
                                                       h * 512 + j * 128 + 128],
                                                    rb[h][:, tc0:tc0 + 512],
                                                    start=st, stop=sp),
                                           reads=[bwt, rb_bufs[h]], writes=wr)
                                ic = IS * 4 + j
                                if phase == "g":
                                    k.v.do(lambda e, ic=ic, psv=psv, tc0=tc0, bank=bank:
                                           e.tensor_tensor(
                                               gubuf[:, ic * 512:(ic + 1) * 512],
                                               psv[0:128, 0:512],
                                               qg_b[:, tc0:tc0 + 512], OP.mult),
                                           reads=[bank, b_qgb], writes=[b_gub])
                                    k.s.do(lambda e, ic=ic: e.activation(
                                        gubuf[:, ic * 512:(ic + 1) * 512],
                                        gubuf[:, ic * 512:(ic + 1) * 512], AF.Silu),
                                        reads=[b_gub], writes=[b_gub])
                                else:
                                    k.v.do(lambda e, ic=ic, psv=psv, bank=bank: e.tensor_tensor(
                                        gubuf[:, ic * 512:(ic + 1) * 512],
                                        gubuf[:, ic * 512:(ic + 1) * 512],
                                        psv[0:128, 0:512], OP.mult),
                                        reads=[bank], writes=[b_gub])
                    for ic in range(32):
                        if ic == 0:
                            k.s.do(lambda e: e.activation(sB3[:, :], gubuf[:, 0:512], AF.Abs),
                                   reads=[b_gub], writes=[b_sB3])
                        else:
                            k.s.do(lambda e, ic=ic: e.activation(
                                sF3[:, :], gubuf[:, ic * 512:(ic + 1) * 512], AF.Abs),
                                reads=[b_gub], writes=[b_sF3])
                            k.v.do(lambda e: e.tensor_tensor(
                                sB3[:, :], sB3[:, :], sF3[:, :], OP.max),
                                reads=[b_sF3], writes=[b_sB3])
                    for ti in range(4):
                        c0 = ti * 128
                        k.t.do(lambda e, c0=c0: e.transpose(
                            pb[7][0:128, 0:128], sB3[:, c0:c0 + 128], ident[:, :]),
                            reads=[b_sB3, b_ident], writes=[b_pb[7]])
                        k.v.do(lambda e: e.reduce_max(sF3[0:128, 0:1], pb[7][0:128, 0:128],
                                                      axis=AX.X),
                               reads=[b_pb[7]], writes=[b_sF3])
                        k.t.do(lambda e: e.transpose(
                            pb[7][0:1, 0:128], sF3[0:128, 0:1], ident[:, :]),
                            reads=[b_sF3, b_ident], writes=[b_pb[7]])
                        k.v.do(lambda e, c0=c0: e.tensor_copy(
                            rows[0:1, RS[3] + c0:RS[3] + c0 + 128], pb[7][0:1, 0:128]),
                            reads=[b_pb[7]], writes=[b_rows])
                    k.v.do(lambda e, tc0=tc0: e.tensor_scalar(
                        rows[0:1, RS[4]:RS[4] + 512],
                        rows[0:1, sslot + tc0:sslot + tc0 + 512], 1e-5, None, OP.mult),
                        reads=[b_rows], writes=[b_rows])
                    k.v.do(lambda e: e.tensor_copy(rows[0:1, RS[5]:RS[5] + 1],
                                                   mw_rows[0:1, 8 + ui:9 + ui]),
                           reads=[b_mwrows], writes=[b_rows])
                    k.v.do(lambda e: e.reciprocal(rows[0:1, RS[5]:RS[5] + 1],
                                                  rows[0:1, RS[5]:RS[5] + 1]),
                           reads=[b_rows], writes=[b_rows])
                    k.v.do(lambda e: e.tensor_scalar(
                        rows[0:1, RS[4]:RS[4] + 512], rows[0:1, RS[4]:RS[4] + 512],
                        rows[0:1, RS[5]:RS[5] + 1], None, OP.mult),
                        reads=[b_rows], writes=[b_rows])
                    k.v.do(lambda e: e.tensor_tensor(
                        rows[0:1, RS[3]:RS[3] + 512], rows[0:1, RS[3]:RS[3] + 512],
                        rows[0:1, RS[4]:RS[4] + 512], OP.max),
                        reads=[b_rows], writes=[b_rows])
                    k.v.do(lambda e: e.reciprocal(rows[0:1, RS[4]:RS[4] + 512],
                                                  rows[0:1, RS[3]:RS[3] + 512]),
                           reads=[b_rows], writes=[b_rows])
                    k.v.do(lambda e: e.tensor_scalar_mul(
                        rows[0:1, RS[4]:RS[4] + 512], rows[0:1, RS[4]:RS[4] + 512], 127.0),
                        reads=[b_rows], writes=[b_rows])
                    k.v.do(lambda e, tc0=tc0: e.tensor_tensor(
                        rows[0:1, RS[5]:RS[5] + 512],
                        rows[0:1, islot + tc0:islot + tc0 + 512],
                        rows[0:1, RS[3]:RS[3] + 512], OP.mult),
                        reads=[b_rows], writes=[b_rows])
                    k.v.do(lambda e: e.tensor_scalar(
                        rows[0:1, RS[5]:RS[5] + 512], rows[0:1, RS[5]:RS[5] + 512],
                        mw_rows[0:1, 8 + di_:9 + di_], 1.0 / 127.0, OP.mult, OP.mult),
                        reads=[b_rows, b_mwrows], writes=[b_rows])
                    k.v.do(lambda e: e.tensor_scalar(
                        rows[0:1, RS[5]:RS[5] + 512], rows[0:1, RS[5]:RS[5] + 512],
                        mw_rows[0:1, 8 + ui:9 + ui], None, OP.mult),
                        reads=[b_rows, b_mwrows], writes=[b_rows])
                    k.t.do(lambda e: e.matmul(pb[7][0:128, 0:512], ones_f[0:1, 0:128],
                                              rows[0:1, RS[4]:RS[4] + 512],
                                              start=True, stop=True),
                           reads=[b_rows, b_ones], writes=[b_pb[7]])
                    k.v.do(lambda e: e.tensor_copy(sig_b[:, :], pb[7][0:128, 0:512]),
                           reads=[b_pb[7]], writes=[b_sigb])
                    k.t.do(lambda e: e.matmul(pb[7][0:128, 0:512], ones_f[0:1, 0:128],
                                              rows[0:1, RS[5]:RS[5] + 512],
                                              start=True, stop=True),
                           reads=[b_rows, b_ones], writes=[b_pb[7]])
                    k.v.do(lambda e: e.tensor_copy(osc_b[:, :], pb[7][0:128, 0:512]),
                           reads=[b_pb[7]], writes=[b_oscb])
                    tdd = tg_out[(L, "d")]
                    for icg in range(32):
                        wd, bwd = (wd_a, b_wda) if icg % 2 == 0 else (wd_b, b_wdb)
                        gq, bgq = (guq_a, b_guqa) if icg % 2 == 0 else (guq_b, b_guqb)
                        dwds[0 if icg % 2 == 0 else 1].do(
                            lambda e, icg=icg, wd=wd: e.dma_start(
                                out=wd[:, :], in_=tdd.ap()[icg * 128:(icg + 1) * 128, :]),
                            writes=[bwd])
                        k.v.do(lambda e, icg=icg: e.tensor_tensor(
                            sF3[:, :], gubuf[:, icg * 512:(icg + 1) * 512], sig_b[:, :],
                            OP.mult), reads=[b_gub, b_sigb], writes=[b_sF3])
                        k.v.do(lambda e: e.tensor_copy(sI3[:, :], sF3[:, :]),
                               reads=[b_sF3], writes=[b_sI3])
                        k.v.do(lambda e, gq=gq: e.tensor_copy(gq[:, :], sI3[:, :]),
                               reads=[b_sI3], writes=[bgq])
                        for oc in range(8):
                            st, sp = (icg == 0), (icg == 31)
                            wr = [b_pb[oc]] if (st or sp) else []
                            k.t.do(lambda e, wd=wd, oc=oc, gq=gq, st=st, sp=sp: e.matmul(
                                pb[oc][0:128, 0:512], wd[:, oc * 128:(oc + 1) * 128],
                                gq[:, :], start=st, stop=sp),
                                reads=[bwd, bgq], writes=wr)
                    for oc in range(8):
                        k.v.do(lambda e, oc=oc: e.tensor_tensor(
                            dtmp[:, oc * 512:(oc + 1) * 512], pb[oc][0:128, 0:512],
                            osc_b[:, :], OP.mult),
                            reads=[b_pb[oc], b_oscb], writes=[b_dtmp])
                    dstap = adims(outdram.ap(), [[2048, 128], [128 * 2048, 8], [1, 512]],
                                  off=rt0)
                    dfo.do(lambda e, dstap=dstap: e.dma_start(
                        out=dstap, in_=fdims(dtmp[:, :], [[512, 8], [1, 512]])),
                        reads=[b_dtmp], writes=[b_outdram])
            k.fence()

        b_ffnout = Buf("ffnout")
        if run_ffn0:
            ffn_pass(0, RS[0], RS[1], ffnout, b_ffnout)

        if upto == "ffn0":
            with ExitStack() as scd:
                dbg = [scd.enter_context(nc.sbuf_tensor(f"dbg{i}", [128, 2048], F32))
                       for i in range(DCH)]
                b_dbg = bufs("dbg", DCH)
                ddbg = k.dslot("dbg")
                for ch in range(DCH):
                    ddbg.do(lambda e, ch=ch: e.dma_start(
                        out=dbg[ch][:, :], in_=ffnout.ap()[ch * 128:(ch + 1) * 128, :]),
                        extra=[k.dslot("fo0").tag()], writes=[b_dbg[ch]])
                dbg_all = ddbg.tag()
                dso = k.dslot("so")
                for blk in range(2):
                    for ti in range(8):
                        rr = blk * 1024 + ti * 128
                        for ch in range(DCH):
                            pi = 4 + ch % 2
                            k.t.do(lambda e, ch=ch, pi=pi, rr=rr: e.transpose(
                                pb[pi][0:128, 0:128], dbg[ch][:, rr:rr + 128], ident[:, :]),
                                reads=[b_dbg[ch], b_ident], writes=[b_pb[pi]],
                                extra=[dbg_all]),
                            k.v.do(lambda e, ch=ch, pi=pi: e.tensor_copy(
                                ldtmp[:, ch * 128:(ch + 1) * 128], pb[pi][0:128, 0:128]),
                                reads=[b_pb[pi]], writes=[b_ldtmp])
                        dso.do(lambda e, ro=blk * BT + ti * 128: e.dma_start(
                            out=out[ro:ro + 128, :], in_=ldtmp[:, :]), reads=[b_ldtmp])
                k.fence()


        b_kg, b_vg = Buf("kg"), Buf("vg")
        if run_l1prep:
            b_x1p = Buf("x1p")
            with ExitStack() as sl1:
                xn1b = [sl1.enter_context(nc.sbuf_tensor(f"xn1b{i}", [128, 2048], BF16))
                        for i in range(DCH)]
                b_xn1b = bufs("xn1b", DCH)
                slb = ExitStack()
                x1b = [slb.enter_context(nc.sbuf_tensor(
                    f"x1b_{i}", [128, 1024], F32)) for i in range(DCH)]
                b_x1b = bufs("x1b", DCH)
                xn1f = [slb.enter_context(nc.sbuf_tensor(
                    f"xn1f_{i}", [128, 1024], F32)) for i in range(DCH)]
                b_xn1f = bufs("xn1f", DCH)
                r1 = [slb.enter_context(nc.sbuf_tensor(
                    f"r1_{i}", [128, 1024], BF16)) for i in range(DCH)]
                b_r1 = bufs("r1", DCH)
                tmt = [slb.enter_context(nc.sbuf_tensor(
                    f"tmt{i}", [128, 1024], F32)) for i in range(2)]
                b_tmt = bufs("tmt", 2)
                fst = [slb.enter_context(nc.sbuf_tensor(
                    f"fst{i}", [128, 1024], F32)) for i in range(2)]
                b_fst = bufs("fst", 2)
                scrB2 = slb.enter_context(nc.sbuf_tensor("scrB2", [128, 1536], F32))
                b_scrB2 = Buf("scrB2")
                sA2 = slb.enter_context(nc.sbuf_tensor("sA2", [128, 1024], F32))
                sF2 = slb.enter_context(nc.sbuf_tensor("sF2", [128, 512], F32))
                sI2 = slb.enter_context(nc.sbuf_tensor("sI2", [128, 512], I32))
                b_sA2, b_sF2, b_sI2 = Buf("sA2"), Buf("sF2"), Buf("sI2")
                for blk in range(2):
                    boff = B0OFF if blk == 0 else B1OFF
                    if True:
                        dmx = [k.dslot("l1ma"), k.dslot("l1mb")]
                        dfo = [k.dslot("l1fa"), k.dslot("l1fb")]
                        dtm = [k.dslot("l1ta"), k.dslot("l1tb")]
                        for ch in range(DCH):
                            dmx[ch % 2].do(lambda e, ch=ch, blk=blk: e.dma_start(
                                out=x1b[ch][:, :],
                                in_=mixout.ap()[ch * 128:(ch + 1) * 128,
                                                blk * 1024:(blk + 1) * 1024]),
                                extra=[k.dslot("mix").tag()], writes=[b_x1b[ch]])
                        for par in range(2):
                            _mt = dmx[par].tag()
                            for ch in range(par, DCH, 2):
                                b_x1b[ch].w = _mt
                        for ch in range(DCH):
                            dfo[ch % 2].do(lambda e, ch=ch, blk=blk: e.dma_start(
                                out=fst[ch % 2][:, :],
                                in_=ffnout.ap()[ch * 128:(ch + 1) * 128,
                                                blk * 1024:(blk + 1) * 1024]),
                                extra=[k.dslot("fo0").tag()], writes=[b_fst[ch % 2]])
                            k.v.do(lambda e, ch=ch: e.tensor_tensor(
                                x1b[ch][:, :], x1b[ch][:, :], fst[ch % 2][:, :], OP.add),
                                reads=[b_fst[ch % 2]], writes=[b_x1b[ch]])
                        for ti in range(8):
                            par = ti % 2
                            dtm[par].do(lambda e, blk=blk, ti=ti, par=par: e.dma_start(
                                out=tmt[par][:, :],
                                in_=xpark.ap()[blk * 1024 + ti * 128:
                                               blk * 1024 + (ti + 1) * 128, :]),
                                extra=[k.dslot("pka").tag(), k.dslot("pkb").tag()],
                                writes=[b_tmt[par]])
                            for ch in range(DCH):
                                pi = 6 + ch % 2
                                k.t.do(lambda e, ch=ch, pi=pi, par=par: e.transpose(
                                    pb[pi][0:128, 0:128], tmt[par][:, ch * 128:(ch + 1) * 128],
                                    ident[:, :]),
                                    reads=[b_tmt[par], b_cst], writes=[b_pb[pi]])
                                k.v.do(lambda e, ch=ch, pi=pi, ti=ti: e.tensor_tensor(
                                    x1b[ch][:, ti * 128:(ti + 1) * 128],
                                    x1b[ch][:, ti * 128:(ti + 1) * 128],
                                    pb[pi][0:128, 0:128], OP.add),
                                    reads=[b_pb[pi]], writes=[b_x1b[ch]])
                        dx1 = k.dslot("x1p")
                        for ch in range(DCH):
                            dx1.do(lambda e, ch=ch, blk=blk: e.dma_start(
                                out=x1park.ap()[ch * 128:(ch + 1) * 128,
                                                blk * 1024:(blk + 1) * 1024],
                                in_=x1b[ch][:, :]), reads=[b_x1b[ch]], writes=[b_x1p])
                        layernorm(b_x1b, x1b, b_xn1f, xn1f, scrB2, b_scrB2, ncols=1024,
                                  SM=RS[2], SV=RS[2] + 1024, ST=RS[3])
                        for ch in range(DCH):
                            k.v.do(lambda e, ch=ch, blk=blk: e.tensor_copy(
                                xn1b[ch][:, blk * 1024:(blk + 1) * 1024], xn1f[ch][:, :]),
                                reads=[b_xn1f[ch]], writes=[b_xn1b[ch]])
                        act_quant(b_xn1f, xn1f, b_r1, r1, RS[0] + boff, RS[1] + boff,
                                  sA2, b_sA2, sF2, b_sF2, sI2, b_sI2, ncols=1024)
                        dr1 = k.dslot("r1p")
                        for ch in range(DCH):
                            dr1.do(lambda e, ch=ch, boff=boff: e.dma_start(
                                out=rpark.ap()[ch * 128:(ch + 1) * 128, boff:boff + 1024],
                                in_=r1[ch][:, :]), reads=[b_r1[ch]], writes=[b_rp])
                    k.fence()
                slb.close()

                if run_qkv:
                    with ExitStack() as slq:
                        wqk_sb = slq.enter_context(nc.sbuf_tensor("wqk_sb", [128, 1024], BF16))
                        wv_sb = slq.enter_context(nc.sbuf_tensor("wv_sb", [128, 8192], BF16))
                        qkv_sb = slq.enter_context(nc.sbuf_tensor("qkv_sb", [128, 8], F32))
                        bvr = slq.enter_context(nc.sbuf_tensor("bvr", [1, 1024], F32))
                        bvt = slq.enter_context(nc.sbuf_tensor("bvt", [128, 1024], F32))
                        qks = [slq.enter_context(nc.sbuf_tensor(f"qks{i}", [128, 512], F32))
                               for i in range(2)]
                        qb = slq.enter_context(nc.sbuf_tensor("qb", [128, 2048], BF16))
                        kb_t = slq.enter_context(nc.sbuf_tensor("kb_t", [128, 2048], BF16))
                        vs = [slq.enter_context(nc.sbuf_tensor(f"vs{i}", [128, 512], F32))
                              for i in range(2)]
                        vtile = [slq.enter_context(nc.sbuf_tensor(f"vtile{i}", [128, 1024], BF16))
                                 for i in range(2)]
                        b_wqk, b_wv, b_qkv, b_bvr, b_bvt = (Buf("wqk"), Buf("wv"), Buf("qkv"),
                                                            Buf("bvr"), Buf("bvt"))
                        b_qks = bufs("qks", 2)
                        b_qb, b_kbt = Buf("qb"), Buf("kbt")
                        b_vs = bufs("vs", 2)
                        b_vtile = bufs("vtile", 2)
                        k.dslot("wqk2", "gpsimd").do(lambda e: e.dma_start(
                            out=wqk_sb[:, :],
                            in_=adims(wqk_in, [[128, 128], [128 * 128, 8], [1, 128]])),
                            writes=[b_wqk])
                        k.dslot("wv2", "gpsimd").do(lambda e: e.dma_start(
                            out=wv_sb[:, :],
                            in_=adims(wv_in, [[D, 128], [128 * D, 8], [1, 1024]])),
                            writes=[b_wv])
                        k.dslot("qkv2").do(lambda e: e.dma_start(
                            out=qkv_sb[:, :], in_=qkvec_in[:, :]), writes=[b_qkv])
                        k.dslot("qkv2").do(lambda e: e.dma_start(
                            out=bvr[:, :], in_=bvrow_in[:, :]), writes=[b_bvr])
                        _qt = k.dslot("qkv2").tag()
                        b_qkv.w = _qt
                        b_bvr.w = _qt
                        for hh in range(2):
                            k.t.do(lambda e, hh=hh: e.matmul(
                                pb[6][0:128, 0:512], ones_f[0:1, 0:128],
                                bvr[0:1, hh * 512:(hh + 1) * 512], start=True, stop=True),
                                reads=[b_bvr, b_cst], writes=[b_pb[6]])
                            k.v.do(lambda e, hh=hh: e.tensor_copy(
                                bvt[:, hh * 512:(hh + 1) * 512], pb[6][0:128, 0:512]),
                                reads=[b_pb[6]], writes=[b_bvt])
                        for tsl in range(4):
                            for ch in range(DCH):
                                st, sp = (ch == 0), (ch == DCH - 1)
                                wr = [b_pb[4]] if (st or sp) else []
                                k.t.do(lambda e, ch=ch, tsl=tsl, st=st, sp=sp: e.matmul(
                                    pb[4][0:128, 0:512], wqk_sb[:, ch * 128:(ch + 1) * 128],
                                    xn1b[ch][:, tsl * 512:(tsl + 1) * 512],
                                    start=st, stop=sp),
                                    reads=[b_wqk, b_xn1b[ch]], writes=wr)
                            k.s.do(lambda e, tsl=tsl: e.activation(
                                qks[tsl % 2][:, :], pb[4][0:128, 0:512], AF.Silu,
                                bias=qkv_sb[:, 0:1], scale=1.0),
                                reads=[b_pb[4], b_qkv], writes=[b_qks[tsl % 2]])
                            k.v.do(lambda e, tsl=tsl: e.tensor_scalar(
                                qb[:, tsl * 512:(tsl + 1) * 512], qks[tsl % 2][:, :],
                                qkv_sb[:, 1:2], qkv_sb[:, 2:3], OP.mult, OP.add),
                                reads=[b_qks[tsl % 2], b_qkv], writes=[b_qb])
                            k.v.do(lambda e, tsl=tsl: e.tensor_scalar(
                                kb_t[:, tsl * 512:(tsl + 1) * 512], qks[tsl % 2][:, :],
                                qkv_sb[:, 3:4], qkv_sb[:, 4:5], OP.mult, OP.add),
                                reads=[b_qks[tsl % 2], b_qkv], writes=[b_kbt])
                        dqp = k.dslot("qp2")
                        dkp = k.dslot("kp2")
                        dqp.do(lambda e: e.dma_start(out=qpark.ap()[:, :], in_=qb[:, :]),
                               reads=[b_qb])
                        dkp.do(lambda e: e.dma_start(out=kin.ap()[:, :], in_=kb_t[:, :]),
                               reads=[b_kbt])
                        dvp = [k.dslot("vp2a"), k.dslot("vp2b")]
                        for gt in range(16):
                            t0 = gt * 128
                            par = gt % 2
                            for vh in range(2):
                                for ch in range(DCH):
                                    st, sp = (ch == 0), (ch == DCH - 1)
                                    wr = [b_pb[vh]] if (st or sp) else []
                                    k.t.do(lambda e, ch=ch, vh=vh, t0=t0, st=st, sp=sp:
                                           e.matmul(
                                               pb[vh][0:128, 0:512],
                                               xn1b[ch][:, t0:t0 + 128],
                                               wv_sb[:, ch * 1024 + vh * 512:
                                                     ch * 1024 + (vh + 1) * 512],
                                               start=st, stop=sp),
                                           reads=[b_xn1b[ch], b_wv], writes=wr)
                                k.v.do(lambda e, vh=vh, par=par: e.tensor_tensor(
                                    vs[par][:, :], pb[vh][0:128, 0:512],
                                    bvt[:, vh * 512:(vh + 1) * 512], OP.add),
                                    reads=[b_pb[vh], b_bvt], writes=[b_vs[par]])
                                k.s.do(lambda e, vh=vh, par=par: e.activation(
                                    vtile[par][:, vh * 512:(vh + 1) * 512], vs[par][:, :],
                                    AF.Silu),
                                    reads=[b_vs[par]], writes=[b_vtile[par]])
                            dvp[par].do(lambda e, t0=t0, par=par: e.dma_start(
                                out=vin.ap()[t0:t0 + 128, :], in_=vtile[par][:, :]),
                                reads=[b_vtile[par]])
                    k.fence()
                    k.g.do(lambda e: e.collective_compute(
                        "AllGather", OP.bypass, replica_groups=[list(range(NCORES))],
                        ins=[kin.ap().opt()], outs=[kgath.ap().opt()]),
                        extra=[k.dslot("kp2").tag()], writes=[b_kg])
                    k.g.do(lambda e: e.collective_compute(
                        "AllGather", OP.bypass, replica_groups=[list(range(NCORES))],
                        ins=[vin.ap().opt()], outs=[vgath.ap().opt()]),
                        extra=[k.dslot("vp2a").tag(), k.dslot("vp2b").tag()], writes=[b_vg])
            k.fence()

        if upto == "l1prep":
            dso = k.dslot("so")
            for blk in range(2):
                dso.do(lambda e, blk=blk: e.dma_start(
                    out=out[blk * 1024:(blk + 1) * 1024, :],
                    in_=x1park.ap()[:, blk * 1024:(blk + 1) * 1024]),
                    extra=[k.dslot("x1p").tag()])

        if upto == "dqkv":
            dso = k.dslot("so", "gpsimd")
            dso.do(lambda e: e.dma_start(out=out[0:128, :],
                                         in_=adims(qpark.ap(), [[2048, 128], [1, 1024]])),
                   extra=[k.dslot("qp2").tag()])
            dso.do(lambda e: e.dma_start(out=out[128:256, :],
                                         in_=adims(kin.ap(), [[2048, 128], [1, 1024]])),
                   extra=[k.dslot("kp2").tag()])
            dso.do(lambda e: e.dma_start(out=out[1024:2048, :], in_=vin.ap()[0:1024, :]),
                   extra=[k.dslot("vp2a").tag(), k.dslot("vp2b").tag()])


        if run_att:
            SCALE_T = float(T) * float(np.sqrt(128.0))
            with ExitStack() as sat:
                battn = sat.enter_context(nc.sbuf_tensor("battn", [128, 1536], F32))
                t0t = sat.enter_context(nc.sbuf_tensor("t0t", [128, 128], F32))
                t1t = sat.enter_context(nc.sbuf_tensor("t1t", [128, 128], F32))
                tsel_s = sat.enter_context(nc.sbuf_tensor("tsel_s", [128, 48], F32))
                nege = sat.enter_context(nc.sbuf_tensor("nege", [128, 1], F32))
                b_batt, b_t0, b_t1 = Buf("batt"), Buf("t0t"), Buf("t1t")
                b_tsel, b_nege = Buf("tsel"), Buf("nege")
                with ExitStack() as sbm:
                    bmk = sbm.enter_context(nc.sbuf_tensor("bmk", [128, 65 * 128], BF16))
                    relr = sbm.enter_context(nc.sbuf_tensor("relr", [1, 32], F32))
                    relbc = sbm.enter_context(nc.sbuf_tensor("relbc", [128, 32], F32))
                    b31c = sbm.enter_context(nc.sbuf_tensor("b31c", [128, 1], F32))
                    baseb = sbm.enter_context(nc.sbuf_tensor("baseb", [128, 1536], F32))
                    farb = sbm.enter_context(nc.sbuf_tensor("farb", [128, 1536], BF16))
                    b_bmk, b_relr, b_relbc = Buf("bmk"), Buf("relr"), Buf("relbc")
                    b_b31, b_baseb, b_farb = Buf("b31"), Buf("baseb"), Buf("farb")
                    k.dslot("bmk").do(lambda e: e.dma_start(out=bmk[:, :], in_=bmasks_in[:, :]),
                                      writes=[b_bmk])
                    k.dslot("bmk").do(lambda e: e.dma_start(out=relr[:, :], in_=relrow_in[:, :]),
                                      writes=[b_relr])
                    k.dslot("bmk").do(lambda e: e.dma_start(out=baseb[:, :], in_=battb_in[:, :]),
                                      writes=[b_baseb])
                    k.dslot("bmk").do(lambda e: e.dma_start(out=farb[:, :], in_=battf_in[:, :]),
                                      writes=[b_farb])
                    k.dslot("bmk").do(lambda e: e.dma_start(out=tsel_s[:, :], in_=tsel_in[:, :]),
                                      writes=[b_tsel])
                    _bt = k.dslot("bmk").tag()
                    for _b in (b_bmk, b_relr, b_baseb, b_farb, b_tsel):
                        _b.w = _bt
                    k.v.do(lambda e: e.memset(nege[:, :], NEGRAW), writes=[b_nege])
                    k.t.do(lambda e: e.matmul(pb[6][0:128, 0:32], ones_f[0:1, 0:128],
                                              relr[0:1, 0:32], start=True, stop=True),
                           reads=[b_relr, b_cst], writes=[b_pb[6]])
                    k.v.do(lambda e: e.tensor_copy(relbc[:, :], pb[6][0:128, 0:32]),
                           reads=[b_pb[6]], writes=[b_relbc])
                    k.v.do(lambda e: e.tensor_scalar_mul(relbc[:, :], relbc[:, :], SCALE_T),
                           reads=[b_relbc], writes=[b_relbc])
                    k.v.do(lambda e: e.tensor_scalar_mul(b31c[:, :], relbc[:, 31:32], C1),
                           reads=[b_relbc], writes=[b_b31])
                    k.v.do(lambda e: e.memset(t0t[:, :], 0.0), writes=[b_t0])
                    k.v.do(lambda e: e.memset(t1t[:, :], 0.0), writes=[b_t1])
                    for b in range(32):
                        k.v.do(lambda e, b=b: e.scalar_tensor_tensor(
                            t0t[:, :], bmk[:, b * 128:(b + 1) * 128], relbc[:, b:b + 1],
                            t0t[:, :], OP.mult, OP.add),
                            reads=[b_bmk, b_relbc], writes=[b_t0])
                        k.v.do(lambda e, b=b: e.scalar_tensor_tensor(
                            t1t[:, :], bmk[:, (33 + b) * 128:(34 + b) * 128],
                            relbc[:, b:b + 1], t1t[:, :], OP.mult, OP.add),
                            reads=[b_bmk, b_relbc], writes=[b_t1])
                    k.v.do(lambda e: e.scalar_tensor_tensor(
                        t0t[:, :], bmk[:, 32 * 128:33 * 128], nege[:, 0:1], t0t[:, :],
                        OP.mult, OP.add),
                        reads=[b_bmk, b_nege], writes=[b_t0])
                    k.v.do(lambda e: e.scalar_tensor_tensor(
                        battn[:, :], farb[:, :], b31c[:, 0:1], baseb[:, :],
                        OP.mult, OP.add),
                        reads=[b_farb, b_b31, b_baseb], writes=[b_batt])
                k.fence()
                qf = sat.enter_context(nc.sbuf_tensor("qf", [128, 2048], BF16))
                kblk = [sat.enter_context(nc.sbuf_tensor(f"kblk{i}", [128, 1024], BF16))
                        for i in range(2)]
                vblk = [sat.enter_context(nc.sbuf_tensor(f"vblk{i}", [128, 8192], BF16))
                        for i in range(2)]
                attn = [sat.enter_context(nc.sbuf_tensor(f"attn{i}", [128, 256], BF16))
                        for i in range(4)]
                avs = [sat.enter_context(nc.sbuf_tensor(f"avs{i}", [128, 1024], F32))
                       for i in range(2)]
                b_qf = Buf("qf")
                b_kblk = bufs("kblk", 2)
                b_vblk = bufs("vblk", 2)
                b_attn = bufs("attn", 4)
                b_avs = bufs("avs", 2)
                k.dslot("qf").do(lambda e: e.dma_start(out=qf[:, :], in_=qpark.ap()[:, :]),
                                 extra=[k.dslot("qp2").tag()], writes=[b_qf])
                dk2 = [k.dslot("ka2"), k.dslot("kb2")]
                dv2 = [k.dslot(f"vt{j}") for j in range(8)]
                dv2b = [k.dslot(n) for n in ("l1ma", "l1mb", "l1fa", "l1fb",
                                             "l1ta", "l1tb", "x1p", "r1p")]
                b_vbt = [bufs("vbt0", 8), bufs("vbt1", 8)]
                dav = [k.dslot("ava"), k.dslot("avb")]
                for st in range(8):
                    slot = st // 4
                    qc0 = st * 256
                    kbs = list(range(8)) if slot == 0 else list(range(16))
                    nits = len(kbs)
                    for iti, kb in enumerate(kbs):
                        it = kb if slot == 0 else 8 + kb
                        cb = min(kb, 15 - kb)
                        sb = 0 if kb <= 7 else 1
                        par = iti % 2
                        dk2[par].do(lambda e, cb=cb, sb=sb, par=par: e.dma_start(
                            out=kblk[par][:, :],
                            in_=kgath.ap()[cb * 128:(cb + 1) * 128,
                                           sb * 1024:(sb + 1) * 1024]),
                            extra=[b_kg.w] if b_kg.w else [], writes=[b_kblk[par]])
                        for vt in range(8):
                            r0 = cb * 2048 + sb * 1024 + vt * 128
                            (dv2 if par == 0 else dv2b)[vt].do(
                                lambda e, r0=r0, vt=vt, par=par: e.dma_start(
                                    out=vblk[par][:, vt * 1024:(vt + 1) * 1024],
                                    in_=vgath.ap()[r0:r0 + 128, :]),
                                extra=[b_vg.w] if b_vg.w else [], writes=[b_vbt[par][vt]])
                        for kbsub in range(8):
                            spi = 4 + kbsub % 4
                            k.t.do(lambda e, kbsub=kbsub, spi=spi, par=par, qc0=qc0: e.matmul(
                                pb[spi][0:128, 0:256],
                                kblk[par][:, kbsub * 128:(kbsub + 1) * 128],
                                qf[:, qc0:qc0 + 256], start=True, stop=True),
                                reads=[b_kblk[par], b_qf], writes=[b_pb[spi]])
                            s4 = st % 4
                            for qt in range(2):
                                qtg = s4 * 2 + qt
                                tadds = []
                                if kbsub == qtg:
                                    tadds.append((t0t, b_t0, it * 2))
                                if kbsub == qtg - 1:
                                    tadds.append((t1t, b_t1, it * 2))
                                if kbsub == 7 and qtg == 0:
                                    tadds.append((t1t, b_t1, it * 2 + 1))
                                for (tt, b_tt, scol) in tadds:
                                    k.v.do(lambda e, tt=tt, scol=scol, spi=spi, qt=qt:
                                           e.scalar_tensor_tensor(
                                               pb[spi][0:128, qt * 128:(qt + 1) * 128],
                                               tt[:, :], tsel_s[:, scol:scol + 1],
                                               pb[spi][0:128, qt * 128:(qt + 1) * 128],
                                               OP.mult, OP.add),
                                           reads=[b_tt, b_tsel, b_pb[spi]],
                                           writes=[b_pb[spi]])
                            par2 = kbsub % 4
                            for qt in range(2):
                                col = (it * 8 + kbsub) * 8 + (s4 * 2 + qt)
                                k.s.do(lambda e, col=col, spi=spi, qt=qt, par2=par2:
                                       e.activation(
                                           attn[par2][:, qt * 128:(qt + 1) * 128],
                                           pb[spi][0:128, qt * 128:(qt + 1) * 128],
                                           AF.Erf, bias=battn[:, col:col + 1], scale=C1),
                                       reads=[b_pb[spi], b_batt], writes=[b_attn[par2]])
                            k.v.do(lambda e, par2=par2: e.tensor_scalar(
                                attn[par2][:, :], attn[par2][:, :], 0.5, 0.5,
                                OP.mult, OP.add),
                                reads=[b_attn[par2]], writes=[b_attn[par2]])
                            first = (iti == 0 and kbsub == 0)
                            last = (iti == nits - 1 and kbsub == 7)
                            for qt in range(2):
                                for vh in range(2):
                                    bid = qt * 2 + vh
                                    wr = [b_pb[bid]] if (first or last) else []
                                    k.t.do(lambda e, bid=bid, qt=qt, vh=vh, kbsub=kbsub,
                                           par=par, par2=par2, first=first, last=last:
                                           e.matmul(
                                               pb[bid][0:128, 0:512],
                                               attn[par2][:, qt * 128:(qt + 1) * 128],
                                               vblk[par][:, kbsub * 1024 + vh * 512:
                                                         kbsub * 1024 + (vh + 1) * 512],
                                               start=first, stop=last),
                                           reads=[b_attn[par2], b_vbt[par][kbsub]],
                                           writes=wr)
                    for qt in range(2):
                        k.v.do(lambda e, qt=qt: e.tensor_copy(
                            avs[qt][:, 0:512], pb[qt * 2][0:128, 0:512]),
                            reads=[b_pb[qt * 2]], writes=[b_avs[qt]])
                        k.v.do(lambda e, qt=qt: e.tensor_copy(
                            avs[qt][:, 512:1024], pb[qt * 2 + 1][0:128, 0:512]),
                            reads=[b_pb[qt * 2 + 1]], writes=[b_avs[qt]])
                        dav[qt].do(lambda e, qt=qt, qc0=qc0: e.dma_start(
                            out=avpark.ap()[qc0 + qt * 128:qc0 + (qt + 1) * 128, :],
                            in_=avs[qt][:, :]), reads=[b_avs[qt]])
            k.fence()

        if upto == "datt":
            dso = k.dslot("so")
            for r in range(16):
                dso.do(lambda e, r=r: e.dma_start(
                    out=out[r * 128:(r + 1) * 128, :],
                    in_=avpark.ap()[r * 128:(r + 1) * 128, :]),
                    extra=[k.dslot("ava").tag(), k.dslot("avb").tag()])


        if run_final:
            b_f1 = Buf("f1o")
            ffn_pass(1, RS[0], RS[1], ffn1out, b_f1)
            with ExitStack() as sfin:
                dbgf = [sfin.enter_context(nc.sbuf_tensor(f"fin{i}", [128, 2048], F32))
                        for i in range(DCH)]
                b_dbgf = bufs("fin", DCH)
                ffs = [sfin.enter_context(nc.sbuf_tensor(f"ffs{i}", [128, 2048], F32))
                       for i in range(2)]
                b_ffs = bufs("ffs", 2)
                avt = [sfin.enter_context(nc.sbuf_tensor(f"avt{i}", [128, 1024], F32))
                       for i in range(2)]
                b_avt = bufs("avt", 2)
                ldt3 = [sfin.enter_context(nc.sbuf_tensor(f"ldt3{i}", [128, 1024], F32))
                        for i in range(2)]
                b_ldt3 = bufs("ldt3", 2)
                dfx = [k.dslot("fxa"), k.dslot("fxb")]
                dff = [k.dslot("ffa"), k.dslot("ffb")]
                dva = [k.dslot("avta"), k.dslot("avtb")]
                dso2 = [k.dslot("so"), k.dslot("so2")]
                for ch in range(DCH):
                    dfx[ch % 2].do(lambda e, ch=ch: e.dma_start(
                        out=dbgf[ch][:, :], in_=x1park.ap()[ch * 128:(ch + 1) * 128, :]),
                        extra=[k.dslot("x1p").tag()], writes=[b_dbgf[ch]])
                for par in range(2):
                    _ft = dfx[par].tag()
                    for ch in range(par, DCH, 2):
                        b_dbgf[ch].w = _ft
                for ch in range(DCH):
                    dff[ch % 2].do(lambda e, ch=ch: e.dma_start(
                        out=ffs[ch % 2][:, :], in_=ffn1out.ap()[ch * 128:(ch + 1) * 128, :]),
                        extra=[k.dslot("fo1").tag()], writes=[b_ffs[ch % 2]])
                    k.v.do(lambda e, ch=ch: e.tensor_tensor(
                        dbgf[ch][:, :], dbgf[ch][:, :], ffs[ch % 2][:, :], OP.add),
                        reads=[b_ffs[ch % 2]], writes=[b_dbgf[ch]])
                for gt in range(16):
                    rr = gt * 128
                    par = gt % 2
                    dva[par].do(lambda e, rr=rr, par=par: e.dma_start(
                        out=avt[par][:, :], in_=avpark.ap()[rr:rr + 128, :]),
                        extra=[k.dslot("ava").tag(), k.dslot("avb").tag()],
                        writes=[b_avt[par]])
                    for ch in range(DCH):
                        pi = 4 + ch % 2
                        k.t.do(lambda e, ch=ch, pi=pi, rr=rr: e.transpose(
                            pb[pi][0:128, 0:128], dbgf[ch][:, rr:rr + 128], ident[:, :]),
                            reads=[b_dbgf[ch], b_cst], writes=[b_pb[pi]])
                        k.v.do(lambda e, ch=ch, pi=pi, par=par: e.tensor_copy(
                            ldt3[par][:, ch * 128:(ch + 1) * 128], pb[pi][0:128, 0:128]),
                            reads=[b_pb[pi]], writes=[b_ldt3[par]])
                    k.v.do(lambda e, par=par: e.tensor_tensor(
                        ldt3[par][:, :], ldt3[par][:, :], avt[par][:, :], OP.add),
                        reads=[b_avt[par]], writes=[b_ldt3[par]])
                    dso2[par].do(lambda e, rr=rr, par=par: e.dma_start(
                        out=out[rr:rr + 128, :], in_=ldt3[par][:, :]),
                        reads=[b_ldt3[par]])

        for _son in ("so", "so2"):
            if _son in k._dslots and k._dslots[_son].n > 0:
                _soc = k._dslots[_son]
                k.g.do(lambda e, c=_soc, f=_soc.pos(): e.wait_ge(c.sem, f))

        with nc.Block() as block:
            k.build(block)
    return nc


def prep_inputs(inputs):
    x = np.asarray(inputs["L0_attn_x"])[0]
    cst = host_constants(inputs["L0_attn_ln_g"], inputs["L0_attn_ln_b"])
    w_in_h = np.ascontiguousarray(np.asarray(inputs["L0_attn_W_in"], np.float32))
    dt_w_h = np.ascontiguousarray(np.asarray(inputs["L0_attn_dt_w"], np.float32))
    bc_w_h = np.ascontiguousarray(np.concatenate(
        [np.asarray(inputs["L0_attn_B_w"], np.float32),
         np.asarray(inputs["L0_attn_C_w"], np.float32)], axis=1))
    w_out_h = np.ascontiguousarray(np.asarray(inputs["L0_attn_W_out"], np.float32))
    mv = np.zeros((128, 64), np.float32)
    mv[:, 0:8] = np.asarray(inputs["L0_attn_conv_b"], np.float32).reshape(8, 128).T
    mv[:, 8:16] = np.asarray(inputs["L0_attn_dt_b"], np.float32).reshape(8, 128).T
    mv[:, 16:24] = np.asarray(inputs["L0_attn_D_param"], np.float32).reshape(8, 128).T
    mv[:, 24:32] = np.asarray(inputs["L0_attn_log_A"], np.float32).reshape(8, 128).T
    mv[:, 32:64] = np.asarray(inputs["L0_attn_conv_w"], np.float32)[:, 0, :].reshape(
        8, 128, 4).transpose(1, 0, 2).reshape(128, 32)
    import ml_dtypes as _md
    wqk_h = np.ascontiguousarray(np.asarray(inputs["L1_attn_Wqk"], np.float32))
    wv_h = np.ascontiguousarray(np.asarray(inputs["L1_attn_Wv"], np.float32))
    qkv_h = np.zeros((128, 8), np.float32)
    qkv_h[:, 0] = np.asarray(inputs["L1_attn_bqk"], np.float32)
    gam = np.asarray(inputs["L1_attn_gamma"], np.float32)
    bet = np.asarray(inputs["L1_attn_beta"], np.float32)
    qkv_h[:, 1] = gam[0]
    qkv_h[:, 2] = bet[0]
    qkv_h[:, 3] = gam[1]
    qkv_h[:, 4] = bet[1]
    bvrow_h = np.asarray(inputs["L1_attn_bv"], np.float32).reshape(1, D)
    relrow_h = np.asarray(inputs["L1_attn_rel_emb"], np.float32).reshape(1, 32)

    def _bucket(n):
        if n < 16:
            return n
        v = 16 + np.int32(np.log(max(n, 1) / 16.0) / np.log(8.0) * 16.0)
        return min(int(v), 31)
    bm = np.zeros((128, 65 * 128), np.float32)
    for kk in range(128):
        for qq in range(128):
            n = qq - kk
            if n < 0:
                bm[kk, 32 * 128 + qq] = 1.0
            else:
                bm[kk, _bucket(n) * 128 + qq] = 1.0
            n1 = 128 + qq - kk
            bm[kk, (33 + _bucket(n1)) * 128 + qq] = 1.0
    bm = bm.astype(_md.bfloat16)
    NEAR_B = -MU * ERF_SC
    in_maps = []
    for c in range(NCORES):
        b0, b1 = BLOCKS_OF[c]
        sl = np.zeros((2 * BT + 6, D), dtype=np.float32)
        for i, b in enumerate((b0, b1)):
            r0 = b * BT
            if r0 >= 3:
                sl[i * (BT + 3):i * (BT + 3) + 3] = x[r0 - 3:r0]
            sl[i * (BT + 3) + 3:(i + 1) * (BT + 3)] = x[r0:r0 + BT]
        m = {"x_sl": sl}
        m.update(cst)
        for L, gk, uk, dk in ((0, "L0_ffn_w_gate", "L0_ffn_w_up", "L0_ffn_w_down"),
                              (1, "L1_ffn_w_gate", "L1_ffn_w_up", "L1_ffn_w_down")):
            gT = np.ascontiguousarray(np.asarray(inputs[gk]).T)
            uT = np.ascontiguousarray(np.asarray(inputs[uk]).T)
            dT = np.ascontiguousarray(np.asarray(inputs[dk]).T)
            m[f"L{L}gT_sl"] = np.ascontiguousarray(gT[c * 128:(c + 1) * 128])
            m[f"L{L}uT_sl"] = np.ascontiguousarray(uT[c * 128:(c + 1) * 128])
            m[f"L{L}dT_sl"] = np.ascontiguousarray(dT[c * 512:(c + 1) * 512])
        m["W_in"] = w_in_h
        m["dt_w"] = dt_w_h
        m["BC_w"] = bc_w_h
        m["W_out"] = w_out_h
        m["mvec"] = mv
        oh = np.zeros((128, 32), np.float32)
        oh[:, b0] = 1.0
        oh[:, 16 + b1] = 1.0
        m["ohot"] = oh
        m["Wqk"] = wqk_h
        m["Wv"] = wv_h
        m["qkvec"] = qkv_h
        m["bvrow"] = bvrow_h
        m["relrow"] = relrow_h
        m["bmasks"] = bm
        base_h = np.zeros((128, 1536), np.float32)
        far_h = np.zeros((128, 1536), np.float32)
        ts_h = np.zeros((128, 48), np.float32)
        for it in range(24):
            slot = 0 if it < 8 else 1
            kb = it if it < 8 else it - 8
            qblk = b0 if slot == 0 else b1
            ts_h[:, it * 2] = 1.0 if kb == qblk else 0.0
            ts_h[:, it * 2 + 1] = 1.0 if kb == qblk - 1 else 0.0
            for kbsub in range(8):
                for qtg in range(8):
                    col = (it * 8 + kbsub) * 8 + qtg
                    if kb > qblk or (kb == qblk and kbsub > qtg):
                        base_h[:, col] = -1e9
                    elif kb == qblk and kbsub >= qtg - 1:
                        base_h[:, col] = NEAR_B
                    elif kb == qblk - 1 and kbsub == 7 and qtg == 0:
                        base_h[:, col] = NEAR_B
                    else:
                        base_h[:, col] = NEAR_B
                        far_h[:, col] = 1.0
        m["batt_base"] = base_h
        m["batt_far"] = far_h.astype(_md.bfloat16)
        m["tsel"] = ts_h
        in_maps.append(m)
    return in_maps


def kernel(**inputs):
    in_maps = prep_inputs(inputs)
    nc = build_nc(upto=_UPTO[0])
    res = run_bass_kernel_spmd(nc, in_maps, core_ids=list(range(NCORES)))
    outp = np.zeros((T, D), dtype=np.float32)
    for c in range(NCORES):
        b0, b1 = BLOCKS_OF[c]
        o = res.results[c]["out"]
        outp[b0 * BT:(b0 + 1) * BT] = o[0:BT]
        outp[b1 * BT:(b1 + 1) * BT] = o[BT:2 * BT]
    return outp.reshape(1, T, D)


_UPTO = ["full"]
